# version 20
# baseline (speedup 1.0000x reference)
from contextlib import ExitStack

import numpy as np
import ml_dtypes

import concourse.bass as bass
import concourse.mybir as mybir
from concourse.bass_utils import run_bass_kernel_spmd

F32 = mybir.dt.float32
BF16 = mybir.dt.bfloat16
U32 = mybir.dt.uint32
AX = mybir.AxisListType.X
ALU = mybir.AluOpType
AF = mybir.ActivationFunctionType
BF = ml_dtypes.bfloat16

DM = 1024
NCH = 8
CHUNK = 128
PAIRS = 8
EPS = 1e-5
RG_PAIRS = [[0, 1], [2, 3], [4, 5], [6, 7]]

T = 4096
SUB = 512


def _derived():
    global NB, NTI, NSC, SBK, NSB, GRP, NGT
    NB = T // 512
    NTI = T // 128
    NSC = T // CHUNK
    SBK = 2 * SUB
    NSB = 2 * T // SBK
    GRP = T // 2
    NGT = GRP // 128


def set_sizes(t=4096, sub=512):
    global T, SUB
    T, SUB = t, sub
    _derived()


_derived()


def _build_slopes(n):
    def po2(m):
        start = 2.0 ** (-2.0 ** (-(np.log2(m) - 3)))
        return [start * start ** i for i in range(m)]
    if np.log2(n).is_integer():
        return po2(n)
    closest = 2 ** int(np.floor(np.log2(n)))
    return po2(closest) + _build_slopes(2 * closest)[0::2][:n - closest]


def host_consts():
    c = {}
    c["eye_bf"] = np.eye(128, dtype=np.float32).astype(BF)
    sw = np.zeros((128, 128), np.float32)
    for p in range(128):
        sw[p, p ^ 1] = 1.0
    c["swap_bf"] = sw.astype(BF)
    c["ones_row_bf"] = np.ones((1, 128), np.float32).astype(BF)
    c["ones_col_bf"] = np.ones((128, 1), np.float32).astype(BF)
    c["ones_col_f32"] = np.ones((128, 1), np.float32)
    c["ones_row_f32"] = np.ones((1, 128), np.float32)

    s = (np.array(_build_slopes(16), dtype=np.float32) * np.float32(1.0 + 1e-05)).astype(np.float32)
    idx = np.arange(CHUNK, dtype=np.float32)
    diff = idx[:, None] - idx[None, :]
    maskT = np.zeros((128, 16 * 128), np.float32)
    for h in range(16):
        m = (np.exp(-s[h] * np.maximum(diff, 0.0)) * (diff >= 0)).astype(np.float32)
        maskT[:, h * 128:(h + 1) * 128] = m.T
    c["maskT_c"] = maskT.astype(BF)
    lam_q = np.exp(-s[:, None] * (idx + 1.0)).astype(np.float32)
    lam_k = np.exp(-s[:, None] * (CHUNK - 1.0 - idx)).astype(np.float32)
    lam_c = np.exp(-s * CHUNK).astype(np.float32)
    lamq = np.zeros((128, 8 * 128), np.float32)
    lamk = np.zeros((128, 8 * 128), np.float32)
    lamc = np.zeros((64, 8 * 128), np.float32)
    for p in range(8):
        lamq[0:64, p * 128:(p + 1) * 128] = lam_q[2 * p][None, :]
        lamq[64:128, p * 128:(p + 1) * 128] = lam_q[2 * p + 1][None, :]
        for col in range(128):
            h = 2 * p + col // 64
            lamk[:, p * 128 + col] = lam_k[h]
            lamc[:, p * 128 + col] = lam_c[h]
    c["lamq_c"] = lamq.astype(BF)
    c["lamk_c"] = lamk.astype(BF)
    c["lamc_c"] = lamc
    c["lamc_pow"] = [[float(np.float32(np.float64(lam_c[h]) ** i)) for h in range(16)]
                     for i in range(64)]
    misc = np.zeros((128, 4), np.float32)
    misc[:, 0] = EPS
    misc[:, 1] = 1e-8
    misc[:, 2] = 1e-9
    c["misc_c"] = misc
    return c


CONSTS = host_consts()

CONST_SPECS = [
    ("c_eye", [128, 128], BF16, "eye_bf"),
    ("c_swap", [128, 128], BF16, "swap_bf"),
    ("c_ones_row", [1, 128], BF16, "ones_row_bf"),
    ("c_ones_col", [128, 1], BF16, "ones_col_bf"),
    ("c_ones_col_f", [128, 1], F32, "ones_col_f32"),
    ("c_ones_row_f", [1, 128], F32, "ones_row_f32"),
    ("c_misc", [128, 4], F32, "misc_c"),
]
P1_CONST_SPECS = [
    ("c_maskT", [128, 16 * 128], BF16, "maskT_c"),
    ("c_lamq", [128, 8 * 128], BF16, "lamq_c"),
    ("c_lamk", [128, 8 * 128], BF16, "lamk_c"),
    ("c_lamc", [64, 8 * 128], F32, "lamc_c"),
]


_NOINC = object()


class Ctr:
    __slots__ = ("h", "n", "name")

    def __init__(self, handle, name):
        self.h = handle
        self.n = 0
        self.name = name


class Eng:

    def __init__(self, kb, name):
        self.kb = kb
        self.name = name
        self.ctr = Ctr(kb.ctx.enter_context(kb.nc.semaphore("c_" + name)), name)
        self.waited = {}
        self.q = []

    def _flush(self):
        if not self.q:
            return
        items = self.q
        self.q = []

        def body(e):
            for deps, f, inc in items:
                self._waits(e, deps)
                instr = f(e)
                if inc:
                    instr.then_inc(self.ctr.h, inc)

        getattr(self.kb.block, self.name)(body)

    def _body(self, f):
        self._flush()
        getattr(self.kb.block, self.name)(f)

    def _waits(self, e, deps):
        need = {}
        for d in deps:
            if d is None:
                continue
            ctr, val = d
            if val is None or val <= 0:
                continue
            if self.waited.get(ctr, 0) >= val:
                continue
            need[ctr] = max(need.get(ctr, 0), val)
        for ctr, val in need.items():
            e.wait_ge(ctr.h, val)
            self.waited[ctr] = val

    def op(self, deps, f, inc=1):
        if inc:
            self.ctr.n += inc
        self.q.append((list(deps), f, inc))
        if len(self.q) >= 256:
            self._flush()
        return (self.ctr, self.ctr.n)

    def tok(self):
        return (self.ctr, self.ctr.n)


class Dma:

    def __init__(self, kb, eng, name):
        self.kb = kb
        self.eng = eng
        self.ctr = Ctr(kb.ctx.enter_context(kb.nc.semaphore("d_" + name)), name)

    def start(self, deps, out, in_, **kw):
        ctr = self.ctr
        ctr.n += 16

        def f(e):
            return e.dma_start(out=out, in_=in_, **kw)

        self.eng.q.append((list(deps), lambda e: f(e).then_inc(ctr.h, 16) or _NOINC, 0))
        if len(self.eng.q) >= 256:
            self.eng._flush()
        return (ctr, ctr.n)

    def start_t(self, deps, out, in_):
        ctr = self.ctr
        ctr.n += 16
        self.eng.q.append((list(deps),
                           lambda e: e.dma_start_transpose(out, in_).then_inc(ctr.h, 16) or _NOINC,
                           0))
        if len(self.eng.q) >= 256:
            self.eng._flush()
        return (ctr, ctr.n)


class KBScope(ExitStack):

    def __init__(self, kb):
        super().__init__()
        self.kb = kb

    def __exit__(self, *a):
        for e in self.kb.engs:
            e._flush()
        return super().__exit__(*a)


class KB:
    def __init__(self, nc, ctx):
        self.nc = nc
        self.ctx = ctx
        self.block = ctx.enter_context(nc.Block())
        self.pe = Eng(self, "tensor")
        self.ve = Eng(self, "vector")
        self.se = Eng(self, "scalar")
        self.pl = Eng(self, "gpsimd")
        self.sy = Eng(self, "sync")
        self.engs = [self.pe, self.ve, self.se, self.pl, self.sy]
        self.dW = [Dma(self, self.pl, f"w{i}") for i in range(6)]
        self.dX = [Dma(self, self.sy, f"x{i}") for i in range(2)]
        self.dS = Dma(self, self.sy, "st")
        self.dT = [Dma(self, self.sy, f"tp{i}") for i in range(2)]
        self.dA = [Dma(self, self.pl, f"ac{i}") for i in range(2)]
        self.cc = Ctr(ctx.enter_context(nc.semaphore("cc")), "cc")
        self.consts = {}

    def sb(self, name, shape, dt, stack=None):
        return (stack or self.ctx).enter_context(self.nc.sbuf_tensor(name, shape, dt))

    def ps(self, name, shape, dt, stack=None):
        return (stack or self.ctx).enter_context(self.nc.psum_tensor(name, shape, dt))

    def collective(self, deps, kind, op, ins, outs):
        cc = self.cc
        cc.n += 1
        self.pl.q.append((list(deps),
                          lambda e: e.collective_compute(
                              kind, op, replica_groups=RG_PAIRS,
                              ins=ins, outs=outs).then_inc(cc.h) or _NOINC, 0))
        return (cc, cc.n)

    def scope(self):
        return KBScope(self)

    def barrier(self):
        for e in self.engs:
            e._flush()
        toks = [(e.ctr, e.ctr.n) for e in self.engs]
        toks += [(d.ctr, d.ctr.n) for d in self.dW + self.dX + self.dA + self.dT + [self.dS]]
        toks.append((self.cc, self.cc.n))
        for e in self.engs:
            def body(en, ee=e):
                ee._waits(en, toks)
            e._body(body)
        return toks


def declare_const_drams(nc, p1=True):
    cd = {}
    specs = CONST_SPECS + (P1_CONST_SPECS if p1 else [])
    for name, shape, dt, _ in specs:
        cd[name] = nc.dram_tensor(name, shape, dt, kind="ExternalInput").ap()
    if p1:
        cd["c_m0"] = nc.dram_tensor("c_m0", [64, 1], F32, kind="ExternalInput").ap()
    return cd


def const_in_maps(core_id, p1=True):
    m = {}
    specs = CONST_SPECS + (P1_CONST_SPECS if p1 else [])
    for name, shape, dt, key in specs:
        m[name] = CONSTS[key]
    if p1:
        m["c_m0"] = np.full((64, 1), 1.0 if core_id % 2 == 1 else 0.0, np.float32)
    return m


_KEYMAP = {"c_eye": "eye", "c_swap": "swap", "c_ones_row": "ones_row",
           "c_ones_col": "ones_col", "c_ones_col_f": "ones_col_f",
           "c_ones_row_f": "ones_row_f", "c_maskT": "maskT", "c_lamq": "lamq",
           "c_misc": "misc", "c_lamk": "lamk", "c_lamc": "lamc", "c_m0": "m0"}


def load_consts(kb, cd, p1=True, stack=None):
    tok = None
    for name, shape, dt, _ in CONST_SPECS:
        tile = kb.sb("sb" + name, shape, dt)
        tok = kb.dX[0].start([], tile[:, :], cd[name])
        kb.consts[_KEYMAP[name]] = tile
    if p1:
        tok = load_p1_consts(kb, cd, stack)
    kb.consts["_tok"] = tok
    return tok


def load_p1_consts(kb, cd, stack=None):
    tok = None
    for name, shape, dt, _ in P1_CONST_SPECS:
        tile = kb.sb("sb" + name, shape, dt, stack)
        tok = kb.dX[0].start([], tile[:, :], cd[name])
        kb.consts[_KEYMAP[name]] = tile
    tile = kb.sb("sbc_m0", [64, 1], F32, stack)
    tok = kb.dX[0].start([], tile[:, :], cd["c_m0"])
    kb.consts["m0"] = tile
    return tok


def emit_load_x_fm(kb, A, x_dram):
    st = kb.scope()
    with st:
        xtm = [kb.sb(f"lx_tm{i}", [128, DM], BF16, st) for i in range(2)]
        ld_tok = [None, None]
        tp_tok = [None, None]
        for t in range(NTI):
            par = t % 2
            ld_tok[par] = kb.dW[par].start([tp_tok[par]], xtm[par][:, :],
                                           x_dram[t * 128:(t + 1) * 128, :])
            outap = A.rearrange("p (c t) -> p c t", c=NCH)[:, :, t * 128:(t + 1) * 128]
            tp_tok[par] = kb.dT[par].start_t(
                [ld_tok[par]], outap,
                xtm[par][:, :].rearrange("q (c k) -> q c k", k=128))
    kb.barrier()


def emit_fm_ln(kb, X, gamma_col=None, beta_col=None, out=None, tag="ln"):
    out = out if out is not None else X
    nb = T // 512
    st = kb.scope()
    with st:
        sq = [kb.sb(f"{tag}_sq{i}", [128, 512], BF16, st) for i in range(2)]
        ssum = kb.sb(f"{tag}_su", [1, 512], F32, st)
        ssq = kb.sb(f"{tag}_qq", [1, 512], F32, st)
        rstd = kb.sb(f"{tag}_rs", [1, 512], F32, st)
        nmu = kb.sb(f"{tag}_nm", [1, 512], F32, st)
        tmp = [kb.sb(f"{tag}_tp{i}", [128, 512], F32, st) for i in range(2)]
        tmp2 = kb.sb(f"{tag}_tpp", [128, 512], F32, st)
        rs_sb = [kb.sb(f"{tag}_rsb{i}", [128, 512], F32, st) for i in range(2)]
        nm_sb = [kb.sb(f"{tag}_nsb{i}", [128, 512], F32, st) for i in range(2)]
        sum_ps = kb.ps(f"{tag}_sps", [1, 512], F32, st)
        sq_ps = kb.ps(f"{tag}_qps", [1, 512], F32, st)
        rs_ps = [kb.ps(f"{tag}_rps{i}", [128, 512], F32, st) for i in range(2)]
        nm_ps = [kb.ps(f"{tag}_nps{i}", [128, 512], F32, st) for i in range(2)]

        ev_sum = ev_sq = None
        sq_w = [None, None]
        ap_w = [None, None]
        row_use = None
        for blk in range(nb):
            def f_sum(e, b=blk):
                last = None
                for ch in range(NCH):
                    last = e.matmul(sum_ps[:, :], kb.consts["ones_col"][:, :],
                                    X[:, ch * T + b * 512:ch * T + (b + 1) * 512],
                                    start=(ch == 0), stop=(ch == NCH - 1))
                return last

            mm_sum = kb.pe.op([ev_sum], f_sum)
            ev_sum = kb.ve.op([mm_sum, row_use], lambda e: e.tensor_copy(
                ssum[:, :], sum_ps[:, :]))
            mm_sq = None
            for ch in range(NCH):
                par = ch % 2
                a_tok = kb.se.op([sq_w[par]], lambda e, c=ch, b=blk, pp=par: e.activation(
                    sq[pp][:, :], X[:, c * T + b * 512:c * T + (b + 1) * 512], AF.Square))
                mm_sq = kb.pe.op([a_tok, ev_sq], lambda e, c=ch, pp=par: e.matmul(
                    sq_ps[:, :], kb.consts["ones_col"][:, :], sq[pp][:, :],
                    start=(c == 0), stop=(c == NCH - 1)))
                sq_w[par] = mm_sq
            ev_sq = kb.ve.op([mm_sq, row_use], lambda e: e.tensor_copy(
                ssq[:, :], sq_ps[:, :]))
            t_mu = kb.ve.op([ev_sum], lambda e: e.tensor_scalar_mul(
                ssum[:, :], ssum[:, :], 1.0 / DM))
            t_m2 = kb.ve.op([t_mu], lambda e: e.tensor_tensor(
                nmu[:, :], ssum[:, :], ssum[:, :], ALU.mult))
            t_v1 = kb.ve.op([ev_sq], lambda e: e.tensor_scalar(
                ssq[:, :], ssq[:, :], 1.0 / DM, EPS, ALU.mult, ALU.add))
            t_v2 = kb.ve.op([t_v1, t_m2], lambda e: e.tensor_tensor(
                rstd[:, :], ssq[:, :], nmu[:, :], ALU.subtract))
            t_ln = kb.se.op([t_v2], lambda e: e.activation(rstd[:, :], rstd[:, :], AF.Ln))
            t_rs = kb.se.op([t_ln], lambda e: e.activation(rstd[:, :], rstd[:, :],
                                                           AF.Exp, scale=-0.5))
            t_nmu = kb.ve.op([t_rs, t_mu], lambda e: e.scalar_tensor_tensor(
                nmu[:, :], ssum[:, :], -1.0, rstd[:, :], ALU.mult, ALU.mult))
            par = blk % 2
            bc1 = kb.pe.op([t_rs, t_nmu] + (ap_w[par] or []), lambda e, pp=par: e.matmul(
                rs_ps[pp][:, :], kb.consts["ones_row_f"][:, :], rstd[0:1, :],
                start=True, stop=True))
            bc2 = kb.pe.op([t_rs, t_nmu] + (ap_w[par] or []), lambda e, pp=par: e.matmul(
                nm_ps[pp][:, :], kb.consts["ones_row_f"][:, :], nmu[0:1, :],
                start=True, stop=True))
            row_use = bc2
            cp_rs = kb.se.op([bc1] + (ap_w[par] or []), lambda e, pp=par: e.activation(
                rs_sb[pp][:, :], rs_ps[pp][:, :], AF.Copy))
            cp_nm = kb.se.op([bc2] + (ap_w[par] or []), lambda e, pp=par: e.activation(
                nm_sb[pp][:, :], nm_ps[pp][:, :], AF.Copy))
            last_v = last_p = None
            for ch in range(NCH):
                on_ve = (ch % 2 == 0)
                eng = kb.ve if on_ve else kb.pl
                tb_ = tmp[par] if on_ve else tmp2
                lst = last_v if on_ve else last_p
                rs_src = rs_ps[par] if on_ve else rs_sb[par]
                nm_src = nm_ps[par] if on_ve else nm_sb[par]

                def f_m(e, c=ch, b=blk, tb=tb_, rr=rs_src):
                    sl = slice(c * T + b * 512, c * T + (b + 1) * 512)
                    return e.tensor_tensor(tb[:, :], X[:, sl], rr[:, :], ALU.mult)

                t1 = eng.op([bc1, bc2, cp_rs, cp_nm, lst], f_m)
                if gamma_col is None:
                    def f_a(e, c=ch, b=blk, tb=tb_, nn=nm_src):
                        sl = slice(c * T + b * 512, c * T + (b + 1) * 512)
                        return e.tensor_tensor(out[:, sl], tb[:, :], nn[:, :], ALU.add)
                    res = eng.op([t1], f_a)
                else:
                    t2 = eng.op([t1], lambda e, tb=tb_, nn=nm_src: e.tensor_tensor(
                        tb[:, :], tb[:, :], nn[:, :], ALU.add))
                    def f_a(e, c=ch, b=blk, tb=tb_):
                        sl = slice(c * T + b * 512, c * T + (b + 1) * 512)
                        return e.tensor_scalar(out[:, sl], tb[:, :],
                                               gamma_col[:, c:c + 1], beta_col[:, c:c + 1],
                                               ALU.mult, ALU.add)
                    res = eng.op([t2], f_a)
                if on_ve:
                    last_v = res
                else:
                    last_p = res
            ap_w[par] = [last_v, last_p]
    kb.barrier()


def emit_phase1(kb, A, B, W, kv_in, kv_out):
    cons = kb.consts
    lamc_pow = CONSTS["lamc_pow"]

    st = kb.scope()
    with st:
        qf = kb.sb("p1_qf", [128, T], BF16, st)
        kf = kb.sb("p1_kf", [128, T], BF16, st)
        vf = kb.sb("p1_vf", [128, 2 * 512], BF16, st)
        ktm = kb.sb("p1_ktm", [128, T], BF16, st)
        vtm = kb.sb("p1_vtm", [128, T], BF16, st)
        qsc0 = kb.sb("p1_qsc0", [64, 128], BF16, st)
        qsc1 = kb.sb("p1_qsc1", [64, 128], BF16, st)
        sm0 = kb.sb("p1_sm0", [128, 128], BF16, st)
        sm1 = kb.sb("p1_sm1", [128, 128], BF16, st)
        q20 = kb.sb("p1_q20", [64, 128], BF16, st)
        q21 = kb.sb("p1_q21", [64, 128], BF16, st)
        kv = kb.sb("p1_kv", [64, 128], F32, st)
        kv3 = kb.sb("p1_kv3", [64, 128], F32, st)
        d_all = kb.sb("p1_dall", [64, NSC * 128], BF16, st)
        kvb_all = kb.sb("p1_kvball", [64, NSC * 128], BF16, st)
        kvt = kb.sb("p1_kvt", [64, 128], F32, st)
        kvall = kb.sb("p1_kvall", [64, 8 * 128], F32, st)
        kvp0 = kb.sb("p1_kvp0", [64, 128], F32, st)
        kvp1 = kb.sb("p1_kvp1", [64, 128], F32, st)
        kv32b = kb.sb("p1_kv32b", [64, 8 * 128], BF16, st)
        wbuf = kb.sb("p1_wbuf", [128, 3 * DM], BF16, st)
        wbuf_used = [None]

        pst = kb.scope()
        with pst:
            qkv_ps = [kb.ps(f"p1_qkv{i}", [128, 512], F32, pst) for i in range(2)]
            s_ps = [kb.ps(f"p1_s{i}", [128, 128], F32, pst) for i in range(2)]
            o_ps2 = [kb.ps(f"p1_o{i}", [64, 256], F32, pst) for i in range(2)]
            d_ps = [kb.ps(f"p1_d{i}", [64, 128], F32, pst) for i in range(2)]

            corr_done = None
            prev_m2 = None
            pend = None
            pool_cp_last = None

            def emit_corr(pd):
                pp = pd["p"]
                c_oe = [pd["oe0"], pd["oe1"]]
                co_by = [pd["oe0"], pd["oe1"]]
                cq_last = pd["qe"]
                for i in range(NSC):
                    cq0 = kb.ve.op([pd["qe"], c_oe[0]], lambda e, ii=i, q_=pp:
                                   e.scalar_tensor_tensor(q20[:, :], qf[0:64, ii * 128:(ii + 1) * 128],
                                                          lamc_pow[ii][2 * q_],
                                                          cons["lamq"][0:64, q_ * 128:(q_ + 1) * 128],
                                                          ALU.mult, ALU.mult))
                    cq1 = kb.ve.op([pd["qe"], c_oe[1]], lambda e, ii=i, q_=pp:
                                   e.scalar_tensor_tensor(q21[:, :], qf[64:128, ii * 128:(ii + 1) * 128],
                                                          lamc_pow[ii][2 * q_ + 1],
                                                          cons["lamq"][64:128, q_ * 128:(q_ + 1) * 128],
                                                          ALU.mult, ALU.mult))
                    cpar = i % 2
                    co = kb.pe.op([pd["m2"], cq0, cq1, c_oe[0], c_oe[1], pd["o_tok"],
                                   co_by[cpar]],
                                  lambda e, q_=pp, cp_=cpar: (
                        e.matmul(o_ps2[cp_][:, 0:128], kv32b[:, q_ * 128:q_ * 128 + 64], q20[:, :],
                                 start=True, stop=False),
                        e.matmul(o_ps2[cp_][:, 128:256], kv32b[:, q_ * 128 + 64:q_ * 128 + 128],
                                 q21[:, :], start=False, stop=True))[-1])
                    c_oe[0] = kb.ve.op([co], lambda e, ii=i, q_=pp, cp_=cpar: e.tensor_tensor(
                        B[0:64, q_ * T + ii * 128:q_ * T + (ii + 1) * 128], o_ps2[cp_][:, 0:128],
                        B[0:64, q_ * T + ii * 128:q_ * T + (ii + 1) * 128], ALU.add))
                    c_oe[1] = kb.ve.op([co], lambda e, ii=i, q_=pp, cp_=cpar: e.tensor_tensor(
                        B[64:128, q_ * T + ii * 128:q_ * T + (ii + 1) * 128], o_ps2[cp_][:, 128:256],
                        B[64:128, q_ * T + ii * 128:q_ * T + (ii + 1) * 128], ALU.add))
                    co_by[cpar] = c_oe[1]
                    cq_last = cq1
                return c_oe[0], c_oe[1], cq_last
            sm_tok = [None, None]
            o_tok = None
            oe_tok = [None, None]
            dused_tok = None
            for p in range(PAIRS):
                wb = wbuf
                wt0 = kb.dW[0].start(
                    [wbuf_used[0]],
                    wbuf[:, 0:1536].rearrange("p (g f) -> p g f", f=512),
                    W["wqkv"][p][:, 0:1536].rearrange("p (g f) -> p g f", f=512))
                wt1 = kb.dW[1].start(
                    [wbuf_used[0]],
                    wbuf[:, 1536:3072].rearrange("p (g f) -> p g f", f=512),
                    W["wqkv"][p][:, 1536:3072].rearrange("p (g f) -> p g f", f=512))
                wtoks = [wt0, wt1, wt1]
                wtoks_k = [wt0, wt1]
                pend_o = [pend["o_tok"], pend["oe0"], pend["oe1"]] if pend else []
                ev = {"q": [], "k": [], "v": []}
                se_prev = [None, None]
                tm_toks = []
                ks_toks = []
                vtp_toks = {}
                ev_prev = None

                def emit_qkv_mm(wi, wkey, blk, extra_deps=()):
                    gi = wi * NB + blk
                    par = gi % 2

                    def f_mm(e, w=wi, b=blk, pp=par, w_=wb):
                        last = None
                        for ch in range(NCH):
                            last = e.matmul(
                                qkv_ps[pp][:, :],
                                w_[:, w * DM + ch * 128:w * DM + (ch + 1) * 128],
                                A[:, ch * T + b * 512:ch * T + (b + 1) * 512],
                                start=(ch == 0), stop=(ch == NCH - 1))
                        return last

                    deps = wtoks_k if wkey == "k" else [wtoks[wi]]
                    mm = kb.pe.op(deps + [se_prev[par]] + list(extra_deps), f_mm)
                    wbuf_used[0] = mm
                    if wkey == "v":
                        a_tok = kb.se.op([mm], lambda e, b=blk, pp=par: e.activation(
                            vf[:, (b % 2) * 512:(b % 2 + 1) * 512], qkv_ps[pp][:, :], AF.Silu))
                    else:
                        d = qf if wkey == "q" else kf
                        a_tok = kb.se.op([mm], lambda e, dd=d, b=blk, pp=par: e.activation(
                            dd[:, b * 512:(b + 1) * 512], qkv_ps[pp][:, :], AF.Silu))
                    se_prev[par] = a_tok
                    ev[wkey].append(a_tok)

                for blk in range(NB):
                    emit_qkv_mm(1, "k", blk, pend_o if blk == 0 else ())
                for blk in range(NB):
                    extra = [vtp_toks[blk - 2]] if blk >= 2 else list(pend_o)
                    emit_qkv_mm(2, "v", blk, extra)
                    tk = kb.dT[0].start_t(
                        [ev["k"][blk], ks_toks[-1] if ks_toks else None],
                        ktm[:, blk * 512:(blk + 1) * 512].rearrange("k (c d) -> k c d", d=128),
                        kf[:, blk * 512:(blk + 1) * 512].rearrange("d (c k) -> d c k", k=128))
                    tv = kb.dT[1].start_t(
                        [ev["v"][blk], tm_toks[-1] if tm_toks else None],
                        vtm[:, blk * 512:(blk + 1) * 512].rearrange("k (c d) -> k c d", d=128),
                        vf[:, (blk % 2) * 512:(blk % 2) * 512 + 512].rearrange("d (c k) -> d c k", k=128))
                    vtp_toks[blk] = tv
                    tm_toks.append(tv)
                    kst = kb.ve.op([tk, tv], lambda e, b=blk, pp=p: e.tensor_tensor(
                        ktm[:, b * 512:(b + 1) * 512].rearrange("p (a f) -> p a f", f=128),
                        ktm[:, b * 512:(b + 1) * 512].rearrange("p (a f) -> p a f", f=128),
                        cons["lamk"][:, pp * 128:(pp + 1) * 128].unsqueeze(1).broadcast_to([128, 4, 128]),
                        ALU.mult))
                    ks_toks.append(kst)
                corr_q_last = None
                if pend is not None:
                    a0, a1_, corr_q_last = emit_corr(pend)
                    oe_tok = [a0, a1_]
                    corr_done = a1_
                    pend = None
                for blk in range(NB):
                    emit_qkv_mm(0, "q", blk, [corr_q_last] if corr_q_last else ())
                d_toks = []
                dev_tok = [None, None]
                for i in range(NSC):
                    dpar = i % 2
                    d_tok = kb.pe.op([ks_toks[-1], tm_toks[-1], dev_tok[dpar], o_tok],
                                     lambda e, ii=i, dp=dpar: (
                        e.matmul(d_ps[dp][:, 0:64], ktm[:, ii * 128:ii * 128 + 64],
                                 vtm[:, ii * 128:ii * 128 + 64], start=True, stop=False),
                        e.matmul(d_ps[dp][:, 64:128], ktm[:, ii * 128 + 64:(ii + 1) * 128],
                                 vtm[:, ii * 128 + 64:(ii + 1) * 128], start=False, stop=True))[-1])
                    dev_tok[dpar] = kb.se.op([d_tok], lambda e, ii=i, dp=dpar: e.activation(
                        d_all[:, ii * 128:(ii + 1) * 128], d_ps[dp][:, :], AF.Copy))
                    d_toks.append(dev_tok[dpar])
                kvb_toks = []
                buf3 = [kv, kvt, kv3]
                z0 = kb.ve.op([corr_done, o_tok, pool_cp_last],
                              lambda e: e.memset(kv[:, :], 0.0))
                kv_tok = z0
                for i in range(NSC):
                    cur = buf3[i % 3]
                    nxt = buf3[(i + 1) % 3]
                    kvb_toks.append(kb.pl.op([kv_tok], lambda e, ii=i, cc=cur:
                                             e.tensor_copy(
                        kvb_all[:, ii * 128:(ii + 1) * 128], cc[:, :])))
                    m_t = kb.ve.op([kv_tok, kvb_toks[i - 2] if i >= 2 else None],
                                   lambda e, pp=p, cc=cur, nn=nxt: e.tensor_tensor(
                        nn[:, :], cc[:, :], cons["lamc"][:, pp * 128:(pp + 1) * 128],
                        ALU.mult))
                    kv_tok = kb.ve.op([m_t, d_toks[i]], lambda e, ii=i, nn=nxt:
                                      e.tensor_tensor(
                        nn[:, :], nn[:, :], d_all[:, ii * 128:(ii + 1) * 128], ALU.add))
                pool_cp_last = kvb_toks[-1]
                cp = kb.ve.op([kv_tok], lambda e, pp=p, ff=buf3[NSC % 3]: e.tensor_copy(
                    kvall[:, pp * 128:(pp + 1) * 128], ff[:, :]))
                d1 = kb.dX[1].start([cp], kv_in[p].ap()[:, :],
                                    kvall[:, p * 128:(p + 1) * 128])
                cct = kb.collective([d1], "AllGather", ALU.bypass,
                                    [kv_in[p].ap().opt()], [kv_out[p].ap().opt()])
                l0 = kb.dX[1].start([cct, prev_m2], kvp0[:, :], kv_out[p].ap()[0:64, :])
                l1 = kb.dX[1].start([cct, prev_m2], kvp1[:, :], kv_out[p].ap()[64:128, :])
                m1 = kb.ve.op([l0, l1], lambda e: e.tensor_scalar_mul(
                    kvp0[:, :], kvp0[:, :], cons["m0"][:, :]))
                m2 = kb.ve.op([m1], lambda e, pp=p: e.tensor_copy(
                    kv32b[:, pp * 128:(pp + 1) * 128], kvp0[:, :]))
                prev_m2 = m2
                oe_by = [oe_tok[0], oe_tok[1]]
                sm_tok = [sm_tok[0], sm_tok[1]]
                for i in range(NSC):
                    s0 = kb.pe.op([ev["k"][-1], ev["q"][-1], sm_tok[0]],
                                  lambda e, ii=i: e.matmul(
                        s_ps[0][:, :], kf[0:64, ii * 128:(ii + 1) * 128],
                        qf[0:64, ii * 128:(ii + 1) * 128], start=True, stop=True))
                    s1 = kb.pe.op([ev["k"][-1], ev["q"][-1], sm_tok[1]],
                                  lambda e, ii=i: e.matmul(
                        s_ps[1][:, :], kf[64:128, ii * 128:(ii + 1) * 128],
                        qf[64:128, ii * 128:(ii + 1) * 128], start=True, stop=True))
                    sm_tok[0] = kb.ve.op([s0, o_tok], lambda e, pp=p: e.tensor_tensor(
                        sm0[:, :], s_ps[0][:, :],
                        cons["maskT"][:, (2 * pp) * 128:(2 * pp + 1) * 128], ALU.mult))
                    sm_tok[1] = kb.ve.op([s1, o_tok], lambda e, pp=p: e.tensor_tensor(
                        sm1[:, :], s_ps[1][:, :],
                        cons["maskT"][:, (2 * pp + 1) * 128:(2 * pp + 2) * 128], ALU.mult))
                    qt0 = kb.ve.op([ev["q"][-1], o_tok], lambda e, ii=i, pp=p: e.tensor_tensor(
                        qsc0[:, :], qf[0:64, ii * 128:(ii + 1) * 128],
                        cons["lamq"][0:64, pp * 128:(pp + 1) * 128], ALU.mult))
                    qt1 = kb.ve.op([ev["q"][-1], o_tok], lambda e, ii=i, pp=p: e.tensor_tensor(
                        qsc1[:, :], qf[64:128, ii * 128:(ii + 1) * 128],
                        cons["lamq"][64:128, pp * 128:(pp + 1) * 128], ALU.mult))
                    opar = i % 2
                    o_tok = kb.pe.op(
                        [sm_tok[0], sm_tok[1], kvb_toks[i], qt0, qt1,
                         oe_by[opar]],
                        lambda e, ii=i, op_=opar: (
                            e.matmul(o_ps2[op_][:, 0:128], vtm[:, ii * 128:ii * 128 + 64],
                                     sm0[:, :], start=True, stop=False),
                            e.matmul(o_ps2[op_][:, 0:128], kvb_all[:, ii * 128:ii * 128 + 64],
                                     qsc0[:, :], start=False, stop=False),
                            e.matmul(o_ps2[op_][:, 128:256], vtm[:, ii * 128 + 64:(ii + 1) * 128],
                                     sm1[:, :], start=False, stop=False),
                            e.matmul(o_ps2[op_][:, 128:256], kvb_all[:, ii * 128 + 64:(ii + 1) * 128],
                                     qsc1[:, :], start=False, stop=True))[-1])
                    oe_tok[0] = kb.se.op([o_tok], lambda e, ii=i, pp=p, op_=opar: e.activation(
                        B[0:64, pp * T + ii * 128:pp * T + (ii + 1) * 128],
                        o_ps2[op_][:, 0:128], AF.Copy))
                    oe_tok[1] = kb.se.op([o_tok], lambda e, ii=i, pp=p, op_=opar: e.activation(
                        B[64:128, pp * T + ii * 128:pp * T + (ii + 1) * 128],
                        o_ps2[op_][:, 128:256], AF.Copy))
                    oe_by[opar] = oe_tok[1]
                pend = {"p": p, "m2": m2, "qe": ev["q"][-1],
                        "o_tok": o_tok, "oe0": oe_tok[0], "oe1": oe_tok[1]}
            emit_corr(pend)
    kb.barrier()


def emit_phase1_tail(kb, A, B, W):
    cons = kb.consts
    st = kb.scope()
    with st:
        gb = kb.sb("p1_gb", [128, 16], F32, st)
        t_g = kb.dX[0].start([], gb[:, :], W["gammabeta"])
        kb.barrier()
        emit_fm_ln(kb, B, gamma_col=gb[:, 0:8], beta_col=gb[:, 8:16], tag="oln")

        wo_sb = kb.sb("p1_wo", [128, NCH * DM], BF16, st)
        wg1_sb = kb.sb("p1_wg1", [128, NCH * 64], BF16, st)
        wg2_sb = kb.sb("p1_wg2", [64, DM], BF16, st)
        g1t = kb.sb("p1_g1t", [64, T], BF16, st)
        gt = [kb.sb(f"p1_gt{i}", [128, 512], BF16, st) for i in range(2)]
        og = kb.sb("p1_og", [128, NCH * 512], BF16, st)
        t_wo = kb.dW[0].start([], wo_sb[:, :].rearrange("p (c f) -> p c f", c=NCH),
                              W["wo"].rearrange("(c p) f -> p c f", p=128))
        t_w1 = kb.dW[1].start([], wg1_sb[:, :].rearrange("p (c f) -> p c f", c=NCH),
                              W["wg1"].rearrange("(c p) f -> p c f", p=128))
        t_w2 = kb.dW[2].start([], wg2_sb[:, :], W["wg2"])
        pst = kb.scope()
        with pst:
            g1_ps = kb.ps("p1_g1ps", [64, 512], F32, pst)
            g2_ps = [kb.ps(f"p1_g2ps{i}", [128, 512], F32, pst) for i in range(2)]
            at_ps = [kb.ps(f"p1_atps{i}", [128, 512], F32, pst) for i in range(2)]
            ev_g1 = None
            gt_used = [None, None]
            res_tok = [None, None]
            for blk in range(NB):
                def f_g1(e, b=blk):
                    last = None
                    for ch in range(NCH):
                        last = e.matmul(g1_ps[:, :],
                                        wg1_sb[:, ch * 64:(ch + 1) * 64],
                                        A[:, ch * T + b * 512:ch * T + (b + 1) * 512],
                                        start=(ch == 0), stop=(ch == NCH - 1))
                    return last

                mm_g1 = kb.pe.op([t_w1, ev_g1], f_g1)
                ev_g1 = kb.se.op([mm_g1], lambda e, b=blk: e.activation(
                    g1t[:, b * 512:(b + 1) * 512], g1_ps[:, :], AF.Copy))
                og_toks = []
                for co in range(NCH):
                    par = co % 2
                    mm_g2 = kb.pe.op([t_w2, ev_g1, gt_used[par]], lambda e, c=co, b=blk, pp=par: e.matmul(
                        g2_ps[pp][:, :], wg2_sb[:, c * 128:(c + 1) * 128],
                        g1t[:, b * 512:(b + 1) * 512], start=True, stop=True))
                    sg = kb.se.op([mm_g2], lambda e, pp=par: e.activation(
                        gt[pp][:, :], g2_ps[pp][:, :], AF.Sigmoid))
                    ogt = kb.ve.op([sg], lambda e, c=co, b=blk, pp=par: e.tensor_tensor(
                        og[:, c * 512:(c + 1) * 512], B[:, c * T + b * 512:c * T + (b + 1) * 512],
                        gt[pp][:, :], ALU.mult))
                    gt_used[par] = ogt
                    og_toks.append(ogt)
                for co in range(NCH):
                    par = co % 2

                    def f_at(e, c=co, pp=par):
                        last = None
                        for ci in range(NCH):
                            last = e.matmul(at_ps[pp][:, :],
                                            wo_sb[:, ci * DM + c * 128:ci * DM + (c + 1) * 128],
                                            og[:, ci * 512:(ci + 1) * 512],
                                            start=(ci == 0), stop=(ci == NCH - 1))
                        return last

                    mm_at = kb.pe.op([t_wo] + og_toks + [res_tok[par]], f_at)
                    res_tok[par] = kb.ve.op([mm_at], lambda e, c=co, b=blk, pp=par:
                                            e.scalar_tensor_tensor(
                        B[:, c * T + b * 512:c * T + (b + 1) * 512], at_ps[pp][:, :], 1.0,
                        A[:, c * T + b * 512:c * T + (b + 1) * 512], ALU.mult, ALU.add))
        kb.barrier()
        emit_fm_ln(kb, B, out=A, tag="sln1")


def emit_phase2(kb, A, B, W):
    cons = kb.consts
    st = kb.scope()
    with st:
        wgr = kb.sb("p2_wgr", [128, NCH * DM], BF16, st)
        bgr = kb.sb("p2_bgr", [1, DM], BF16, st)
        ber = kb.sb("p2_ber", [1, 8 * DM], BF16, st)
        web = [kb.sb(f"p2_we{i}", [128, NCH * DM], BF16, st) for i in range(2)]
        lgf = kb.sb("p2_lgf", [128, DM], F32, st)
        zcol = kb.sb("p2_zcol", [128, 1], F32, st)
        pcol = kb.sb("p2_pcol", [128, 1], F32, st)
        f8 = kb.sb("p2_f8", [128, 8], F32, st)
        u8 = kb.sb("p2_u8", [128, 8], U32, st)
        esel = kb.sb("p2_esel", [128, 1], U32, st)
        wcols = kb.sb("p2_wcols", [128, NTI * 8], F32, st)
        rsc = [kb.sb(f"p2_rsc{i}", [128, DM], BF16, st) for i in range(2)]
        accT0 = kb.sb("p2_accT", [128, DM], BF16, st)
        accT = [accT0, accT0]

        t_wgr = kb.dW[0].start([], wgr[:, :].rearrange("p (c f) -> p c f", f=DM),
                               W["wgr"].rearrange("(c p) f -> p c f", p=128))
        t_bg = kb.dW[1].start([], bgr[:, :], W["bg_row"])
        t_be = kb.dW[4].start([], ber[:, :], W["be_flat"])

        pst = kb.scope()
        with pst:
            lg_ps = kb.ps("p2_lg", [128, DM], F32, pst)
            z_ps = [kb.ps(f"p2_z{i}", [128, DM], F32, pst) for i in range(2)]

            ev_tok = None
            wc_tok = None
            for t in range(NTI):
                def f_lg(e, tt=t):
                    last = None
                    for half in range(2):
                        for ch in range(NCH):
                            last = e.matmul(
                                lg_ps[:, half * 512:(half + 1) * 512],
                                A[:, ch * T + tt * 128:ch * T + (tt + 1) * 128],
                                wgr[:, ch * DM + half * 512:ch * DM + (half + 1) * 512],
                                start=(ch == 0), stop=False)
                        last = e.matmul(lg_ps[:, half * 512:(half + 1) * 512],
                                        cons["ones_row"][:, :],
                                        bgr[0:1, half * 512:(half + 1) * 512],
                                        start=False, stop=True)
                    return last

                mm = kb.pe.op([t_wgr, t_bg, ev_tok], f_lg)
                ev_tok = kb.se.op([mm, wc_tok], lambda e: e.activation(
                    lgf[:, :], lg_ps[:, :], AF.Copy))
                nm = kb.ve.op([ev_tok, wc_tok], lambda e: e.reduce_max(
                    pcol[:, :], lgf[:, :], axis=AX, negate=True))
                ez = kb.se.op([nm, ev_tok], lambda e: e.activation(
                    rsc[0][:, :], lgf[:, :], AF.Exp, bias=pcol[:, :],
                    accum_out=zcol[:, :]))
                pz = kb.ve.op([ez], lambda e: e.reciprocal(pcol[:, :], zcol[:, :]))
                m8 = kb.ve.op([pz], lambda e: e.max(f8[:, :], lgf[:, :]))
                ix = kb.ve.op([m8], lambda e: e.max_index(u8[:, :], f8[:, :], lgf[:, :]))
                es = kb.ve.op([ix], lambda e: e.tensor_single_scalar(
                    esel[:, :], u8[:, 0:1], 7, ALU.bitwise_and))
                wc_tok = es
                for ee in range(8):
                    wc_tok = kb.ve.op([wc_tok], lambda e, tt=t, x=ee: e.scalar_tensor_tensor(
                        wcols[:, tt * 8 + x:tt * 8 + x + 1], esel[:, :], float(x),
                        pcol[:, :], ALU.is_equal, ALU.mult))

            we_tok = [None, None]
            mm_last = [None, None]
            add_tok = None
            add_final = {}
            se_tok = [None, None]
            for e_i in range(8):
                par = e_i % 2
                we_tok[par] = kb.dW[2 + par].start(
                    [mm_last[par]],
                    web[par][:, :].rearrange("p (c f) -> p c f", f=DM),
                    W["we"][e_i].rearrange("(c p) f -> p c f", p=128))
                for t in range(NTI):
                    zpar = t % 2

                    def f_z(e, tt=t, pp=par, zz=zpar, ei=e_i):
                        last = None
                        for half in range(2):
                            for ch in range(NCH):
                                last = e.matmul(
                                    z_ps[zz][:, half * 512:(half + 1) * 512],
                                    A[:, ch * T + tt * 128:ch * T + (tt + 1) * 128],
                                    web[pp][:, ch * DM + half * 512:ch * DM + (half + 1) * 512],
                                    start=(ch == 0), stop=False)
                            last = e.matmul(z_ps[zz][:, half * 512:(half + 1) * 512],
                                            cons["ones_row"][:, :],
                                            ber[0:1, ei * DM + half * 512:ei * DM + (half + 1) * 512],
                                            start=False, stop=True)
                        return last

                    mm = kb.pe.op([we_tok[par], t_be, se_tok[zpar], wc_tok], f_z)
                    mm_last[par] = mm
                    if e_i == 0:
                        se_tok[zpar] = kb.se.op([mm], lambda e, tt=t, zz=zpar: e.activation(
                            B[:, tt * DM:(tt + 1) * DM], z_ps[zz][:, :], AF.Relu,
                            scale=wcols[:, tt * 8:tt * 8 + 1]))
                    else:
                        rs = kb.se.op([mm, add_tok], lambda e, tt=t, zz=zpar, ei=e_i, pp=par:
                                      e.activation(rsc[pp][:, :], z_ps[zz][:, :], AF.Relu,
                                                   scale=wcols[:, tt * 8 + ei:tt * 8 + ei + 1]))
                        se_tok[zpar] = rs
                        add_tok = kb.ve.op([rs], lambda e, tt=t, pp=par: e.tensor_tensor(
                            B[:, tt * DM:(tt + 1) * DM], B[:, tt * DM:(tt + 1) * DM],
                            rsc[pp][:, :], ALU.add))
                        if e_i == 7:
                            add_final[t] = add_tok

            res_toks = [None, None]
            for t in range(NTI):
                par = t % 2
                tp_tok = kb.dT[par].start_t(
                    [add_final[t], res_toks[0], res_toks[1]],
                    accT[par][:, :].rearrange("k (c q) -> k c q", q=128),
                    B[:, t * DM:(t + 1) * DM].rearrange("q (c k) -> q c k", k=128))

                def f_res(e, tt=t, pp=par):
                    outap = A.rearrange("p (c t) -> p c t", c=NCH)[:, :, tt * 128:(tt + 1) * 128]
                    return e.scalar_tensor_tensor(
                        outap, accT[pp][:, :].rearrange("p (c t) -> p c t", c=NCH), 1.0,
                        outap, ALU.mult, ALU.add)

                res_toks[par] = kb.ve.op([tp_tok, res_toks[1 - par]], f_res)
    kb.barrier()
    emit_fm_ln(kb, A, tag="sln2")


def p2_weight_drams(nc):
    W = {}
    W["wgr"] = nc.dram_tensor("wgr", [DM, DM], F32, kind="ExternalInput").ap()
    W["bg_row"] = nc.dram_tensor("bg_row", [1, DM], F32, kind="ExternalInput").ap()
    W["we"] = nc.dram_tensor("we", [8, DM, DM], F32, kind="ExternalInput").ap()
    W["be_flat"] = nc.dram_tensor("be_flat", [1, 8 * DM], F32, kind="ExternalInput").ap()
    return W


def p2_weight_maps(inputs):
    return {
        "wgr": np.asarray(inputs["L0_ffn_Wg"], np.float32),
        "bg_row": np.asarray(inputs["L0_ffn_bg"], np.float32).reshape(1, DM),
        "we": np.asarray(inputs["L0_ffn_We"], np.float32),
        "be_flat": np.ascontiguousarray(
            np.asarray(inputs["L0_ffn_be"], np.float32).reshape(1, 8 * DM)),
    }


def build_phase2(dbg=None):
    nc = bass.Bass("TRN2", target_bir_lowering=False)
    cd = declare_const_drams(nc, p1=False)
    s_in = nc.dram_tensor("s_in", [128, NCH * T], BF16, kind="ExternalInput").ap()
    W = p2_weight_drams(nc)
    s_out = nc.dram_tensor("s_out", [128, NCH * T], BF16, kind="ExternalOutput").ap()
    with ExitStack() as ctx:
        kb = KB(nc, ctx)
        A = kb.sb("bufA", [128, NCH * T], BF16)
        B = kb.sb("bufB", [128, NCH * T], BF16)
        load_consts(kb, cd, p1=False)
        kb.dX[1].start([], A[:, :], s_in)
        kb.barrier()
        emit_phase2(kb, A, B, W)
        kb.barrier()
        kb.dS.start([], s_out, A[:, :])
        finish(kb)
    return nc


def emit_phase3(kb, A, W, x2i, x2g):
    cons = kb.consts
    st = kb.scope()
    with st:
        d0 = kb.dX[0].start([], x2i.ap()[:, :], A[:, :])
        cct = kb.collective([d0], "AllGather", ALU.bypass,
                            [x2i.ap().opt()], [x2g.ap().opt()])

        rt = kb.sb("p3_rt", [128, 24 * 128], BF16, st)
        acol = kb.sb("p3_ac", [128, 1], F32, st)
        bcol = kb.sb("p3_bc", [128, 1], F32, st)

        pst0 = kb.scope()
        with pst0:
            rr = kb.sb("p3_rr", [1, DM], F32, pst0)
            n2 = kb.sb("p3_n2", [1, 512], F32, pst0)
            tq = kb.sb("p3_tq", [1, 512], F32, pst0)
            cfull = kb.sb("p3_cf", [1, DM], F32, pst0)
            smfull = kb.sb("p3_sf", [1, DM], F32, pst0)
            ccol = kb.sb("p3_cc", [128, NCH], F32, pst0)
            scol = kb.sb("p3_sc", [128, NCH], F32, pst0)
            sc11 = kb.sb("p3_s11", [1, 2], F32, pst0)
            tcol_ps = kb.ps("p3_tcol", [128, 16], F32, pst0)
            prev = None
            for r in range(3):
                ld = kb.dX[1].start([prev], rr[:, :], W["rotw"][r:r + 1, :])
                t1 = kb.ve.op([ld], lambda e: e.tensor_tensor(
                    n2[:, :], rr[0:1, 0:DM:2], rr[0:1, 0:DM:2], ALU.mult))
                t2 = kb.ve.op([t1], lambda e: e.tensor_tensor(
                    tq[:, :], rr[0:1, 1:DM:2], rr[0:1, 1:DM:2], ALU.mult))
                t3 = kb.ve.op([t2], lambda e: e.tensor_tensor(
                    n2[:, :], n2[:, :], tq[:, :], ALU.add))
                t4 = kb.se.op([t3], lambda e: e.activation(n2[:, :], n2[:, :], AF.Ln))
                t5 = kb.se.op([t4], lambda e: e.activation(n2[:, :], n2[:, :],
                                                           AF.Exp, scale=-0.5))
                c1 = kb.ve.op([t5], lambda e: e.tensor_tensor(
                    tq[:, :], rr[0:1, 0:DM:2], n2[:, :], ALU.mult))
                c2 = kb.ve.op([c1], lambda e: e.tensor_copy(cfull[0:1, 0:DM:2], tq[:, :]))
                c3 = kb.ve.op([c2], lambda e: e.tensor_copy(cfull[0:1, 1:DM:2], tq[:, :]))
                s1 = kb.ve.op([c3], lambda e: e.tensor_tensor(
                    tq[:, :], rr[0:1, 1:DM:2], n2[:, :], ALU.mult))
                s2 = kb.ve.op([s1], lambda e: e.tensor_copy(smfull[0:1, 0:DM:2], tq[:, :]))
                s3 = kb.ve.op([s2], lambda e: e.tensor_scalar_mul(
                    smfull[0:1, 1:DM:2], tq[:, :], -1.0))
                tp1 = kb.pe.op([s3, prev], lambda e: (
                    [e.transpose(tcol_ps[:, ch:ch + 1], cfull[0:1, ch * 128:(ch + 1) * 128],
                                 cons["ones_col_f"][0:1, 0:1]) for ch in range(NCH)]
                    + [e.transpose(tcol_ps[:, 8 + ch:9 + ch], smfull[0:1, ch * 128:(ch + 1) * 128],
                                   cons["ones_col_f"][0:1, 0:1]) for ch in range(NCH)])[-1])
                cc1 = kb.ve.op([tp1], lambda e: e.tensor_copy(ccol[:, :], tcol_ps[:, 0:8]))
                cc2 = kb.ve.op([tp1], lambda e: e.tensor_copy(scol[:, :], tcol_ps[:, 8:16]))
                for ch in range(NCH):
                    e1 = kb.ve.op([cc1, cc2], lambda e, rr_=r, c_=ch: e.tensor_scalar_mul(
                        rt[:, (rr_ * 8 + c_) * 128:(rr_ * 8 + c_ + 1) * 128],
                        cons["eye"][:, :], ccol[:, c_:c_ + 1]))
                    prev = kb.ve.op([e1], lambda e, rr_=r, c_=ch: e.scalar_tensor_tensor(
                        rt[:, (rr_ * 8 + c_) * 128:(rr_ * 8 + c_ + 1) * 128],
                        cons["swap"][:, :], scol[:, c_:c_ + 1],
                        rt[:, (rr_ * 8 + c_) * 128:(rr_ * 8 + c_ + 1) * 128],
                        ALU.mult, ALU.add))
            lsc = kb.dX[1].start([prev], sc11[0:1, 0:1], W["scale"])
            lbi = kb.dX[1].start([lsc], sc11[0:1, 1:2], W["bias"])
            v1 = kb.ve.op([lsc, lbi], lambda e: e.reciprocal(n2[0:1, 0:1], sc11[0:1, 0:1]))
            v2 = kb.ve.op([v1], lambda e: e.tensor_scalar_mul(n2[0:1, 0:1], n2[0:1, 0:1], 2.0))
            v3 = kb.ve.op([v2], lambda e: e.tensor_tensor(
                n2[0:1, 1:2], n2[0:1, 0:1], sc11[0:1, 1:2], ALU.add))
            ab1 = kb.pe.op([v3, tp1], lambda e: (
                e.matmul(tcol_ps[:, 0:1], cons["ones_row_f"][:, :], n2[0:1, 0:1],
                         start=True, stop=True),
                e.matmul(tcol_ps[:, 1:2], cons["ones_row_f"][:, :], n2[0:1, 1:2],
                         start=True, stop=True))[-1])
            ca = kb.ve.op([ab1], lambda e: e.tensor_copy(acol[:, :], tcol_ps[:, 0:1]))
            cb = kb.ve.op([ab1], lambda e: e.tensor_copy(bcol[:, :], tcol_ps[:, 1:2]))
            kb.ve.op([ca, prev], lambda e: e.tensor_scalar_mul(
                rt[:, 0:NCH * 128], rt[:, 0:NCH * 128], acol[:, :]))
        kb.barrier()

        qg = kb.sb("p3_qg", [128, NCH * GRP], BF16, st)
        Og = kb.sb("p3_og", [128, NGT * DM], BF16, st)
        kf3 = kb.sb("p3_kf", [128, NCH * SBK], BF16, st)
        vt3 = kb.sb("p3_vt", [128, (SBK // 128) * DM], BF16, st)
        xk = [kb.sb(f"p3_xk{i}", [128, NCH * SUB], BF16, st) for i in range(2)]
        ptile = kb.sb("p3_pt", [128, SBK], BF16, st)
        ptT = [kb.sb(f"p3_ptt{i}", [128, SBK], BF16, st) for i in range(2)]
        tsf = kb.sb("p3_tsf", [128, SUB], F32, st)
        junk = kb.sb("p3_junk", [128, DM], BF16, st)
        nmst = kb.sb("p3_nmst", [128, NGT], F32, st)
        lst = kb.sb("p3_lst", [128, NGT], F32, st)
        nmj = kb.sb("p3_nmj", [128, 2], F32, st)
        psj = kb.sb("p3_psj", [128, 2], F32, st)
        fj = kb.sb("p3_fj", [128, 2], F32, st)
        alph = kb.sb("p3_alph", [128, 2], F32, st)
        nmnew = kb.sb("p3_nmn", [128, 1], F32, st)
        wv = kb.sb("p3_wv", [128, 2], F32, st)
        wsum = kb.sb("p3_wsum", [128, 1], F32, st)
        s2c = kb.sb("p3_s2c", [128, 1], F32, st)

        pst = kb.scope()
        with pst:
            bd_ps = [kb.ps(f"p3_bd{i}", [128, 512], F32, pst) for i in range(4)]
            vp_ps = kb.ps("p3_vp", [128, DM], F32, pst)

            NKT = SBK // 128
            for g in range(2):
                kb.barrier()
                ev_q = None
                for blk in range(GRP // 512):
                    for ch in range(NCH):
                        par = (blk * NCH + ch) % 2
                        mmq = kb.pe.op([cct, ev_q], lambda e, b=blk, c=ch, pp=par, gg=g: e.matmul(
                            bd_ps[pp][:, :], rt[:, c * 128:(c + 1) * 128],
                            A[:, c * T + gg * GRP + b * 512:c * T + gg * GRP + (b + 1) * 512],
                            start=True, stop=True))
                        ev_q = kb.se.op([mmq], lambda e, b=blk, c=ch, pp=par: e.activation(
                            qg[:, c * GRP + b * 512:c * GRP + (b + 1) * 512],
                            bd_ps[pp][:, :], AF.Copy))
                z1 = kb.ve.op([], lambda e: e.memset(Og[:, :], 0.0))
                z2 = kb.ve.op([z1], lambda e: e.memset(nmst[:, :], 3e38))
                z3 = kb.ve.op([z2], lambda e: e.memset(lst[:, :], 0.0))
                o_up = z3
                ptT_use = None
                vp_use = [None, None]
                ex_pair = [None, None]
                ex_pair_by = [None, None]
                vch3 = None
                mpv_by = [None, None]
                w3_by = [None, None]
                for sb in range(NSB):
                    kv_toks = []
                    for j in range(SBK // SUB):
                        key0 = sb * SBK + j * SUB
                        rank, tt0 = key0 // T, key0 % T
                        xsrc = x2g.ap()[rank * 128:(rank + 1) * 128, :] \
                            .rearrange("p (c t) -> p c t", t=T)[:, :, tt0:tt0 + SUB]
                        lx = kb.dX[j % 2].start(
                            [o_up], xk[j % 2][:, :].rearrange("p (c t) -> p c t", t=SUB), xsrc)
                        ev_k = None
                        for ch in range(NCH):
                            par = ch % 2
                            mmk = kb.pe.op([lx, ev_k, o_up, ex_pair_by[0], ex_pair_by[1]],
                                           lambda e, c=ch, pp=par, jj=j: e.matmul(
                                bd_ps[pp][:, :], rt[:, (8 + c) * 128:(9 + c) * 128],
                                xk[jj % 2][:, c * SUB:(c + 1) * SUB], start=True, stop=True))
                            ev_k = kb.se.op([mmk], lambda e, c=ch, pp=par, jj=j: e.activation(
                                kf3[:, c * SBK + jj * SUB:c * SBK + jj * SUB + SUB],
                                bd_ps[pp][:, :], AF.Copy))
                        kv_toks.append(ev_k)
                        for ks in range(SUB // 128):
                            def f_v(e, jj=j, kk=ks):
                                last = None
                                for c in range(NCH):
                                    last = e.matmul(
                                        vp_ps[:, c * 128:(c + 1) * 128],
                                        xk[jj % 2][:, c * SUB + kk * 128:c * SUB + (kk + 1) * 128],
                                        rt[:, (16 + c) * 128:(17 + c) * 128],
                                        start=True, stop=True)
                                return last

                            mmv = kb.pe.op([lx, o_up, vp_use[0]], f_v)
                            kt = j * (SUB // 128) + ks
                            ev_v = kb.ve.op([mmv], lambda e, k_=kt: e.tensor_copy(
                                vt3[:, k_ * DM:(k_ + 1) * DM], vp_ps[:, :]))
                            vp_use[0] = ev_v
                            kv_toks.append(ev_v)
                    pt_toks = {}
                    for qt in range(NGT + 1):
                        if qt < NGT:
                            pb = qt % 2

                            def f_s(e, q_=qt, b_=pb):
                                last = None
                                for j in range(2):
                                    for c in range(NCH):
                                        last = e.matmul(
                                            bd_ps[2 * b_ + j][:, :],
                                            qg[:, c * GRP + q_ * 128:c * GRP + (q_ + 1) * 128],
                                            kf3[:, c * SBK + j * SUB:c * SBK + (j + 1) * SUB],
                                            start=(c == 0), stop=(c == NCH - 1))
                                return last

                            mms = kb.pe.op(kv_toks + [ex_pair_by[pb], ev_q], f_s)
                            nm0 = kb.ve.op([mms, vch3, ex_pair_by[0], ex_pair_by[1]],
                                           lambda e, b_=pb: e.reduce_max(
                                nmj[:, 0:1], bd_ps[2 * b_][:, :], axis=AX, negate=True))
                            nm1 = kb.ve.op([nm0], lambda e, b_=pb: e.reduce_max(
                                nmj[:, 1:2], bd_ps[2 * b_ + 1][:, :], axis=AX, negate=True))
                            u1 = kb.ve.op([nm1], lambda e: e.tensor_tensor(
                                nmnew[:, :], nmj[:, 0:1], nmj[:, 1:2], ALU.min))
                            u2 = kb.ve.op([u1], lambda e, q_=qt: e.tensor_tensor(
                                nmnew[:, :], nmnew[:, :], nmst[:, q_:q_ + 1], ALU.min))
                            f2 = kb.se.op([u2, o_up], lambda e, q_=qt, b_=pb: e.activation(
                                alph[:, b_:b_ + 1], nmst[:, q_:q_ + 1], AF.Exp, scale=-1.0,
                                bias=nmnew[:, :]))
                            su = kb.ve.op([u2, f2], lambda e, q_=qt: e.tensor_copy(
                                nmst[:, q_:q_ + 1], nmnew[:, :]))
                            ex0 = kb.se.op([u2, pt_toks.get(qt - 1)], lambda e, b_=pb: e.activation(
                                ptile[:, 0:SUB], bd_ps[2 * b_][:, :], AF.Exp,
                                bias=nmnew[:, :], accum_out=psj[:, 0:1]))
                            ex1 = kb.se.op([ex0], lambda e, b_=pb: e.activation(
                                ptile[:, SUB:SBK], bd_ps[2 * b_ + 1][:, :], AF.Exp,
                                bias=nmnew[:, :], accum_out=psj[:, 1:2]))
                            ex_pair_by[pb] = ex1
                            vch3 = su
                            w2 = kb.ve.op([ex1, su], lambda e: e.reduce_sum(
                                wsum[:, :], psj[:, :], axis=AX))
                            w3 = kb.ve.op([w2, f2], lambda e, q_=qt, b_=pb: e.scalar_tensor_tensor(
                                lst[:, q_:q_ + 1], lst[:, q_:q_ + 1], alph[:, b_:b_ + 1],
                                wsum[:, :], ALU.mult, ALU.add))
                            w3_by[pb] = w3
                            pt_toks[qt] = kb.dT[0].start_t(
                                [ex0, ex1, mpv_by[pb]],
                                ptT[pb][:, :].rearrange("k (c q) -> k c q", q=128),
                                ptile[:, :].rearrange("q (c k) -> q c k", k=128))
                        if qt >= 1:
                            q0 = qt - 1
                            b0 = q0 % 2

                            def f_pv(e, b_=b0):
                                last = None
                                for kt in range(NKT):
                                    for half in range(2):
                                        last = e.matmul(
                                            vp_ps[:, half * 512:(half + 1) * 512],
                                            ptT[b_][:, kt * 128:(kt + 1) * 128],
                                            vt3[:, kt * DM + half * 512:kt * DM + (half + 1) * 512],
                                            start=(kt == 0), stop=(kt == NKT - 1))
                                return last

                            mpv = kb.pe.op([pt_toks[q0], o_up, vp_use[0]], f_pv)
                            mpv_by[b0] = mpv
                            o_up = kb.ve.op([mpv, w3_by[b0], o_up], lambda e, q_=q0, b_=b0:
                                            e.scalar_tensor_tensor(
                                Og[:, q_ * DM:(q_ + 1) * DM], Og[:, q_ * DM:(q_ + 1) * DM],
                                alph[:, b_:b_ + 1], vp_ps[:, :], ALU.mult, ALU.add))
                            vp_use[0] = o_up
                ep = None
                for qt in range(NGT):
                    l1 = kb.ve.op([o_up, ep], lambda e, q_=qt: e.reciprocal(
                        s2c[:, :], lst[:, q_:q_ + 1]))
                    q1 = kb.se.op([l1], lambda e, q_=qt: e.activation(
                        junk[:, :], Og[:, q_ * DM:(q_ + 1) * DM], AF.Square,
                        accum_out=wsum[:, :]))
                    i1 = kb.ve.op([q1], lambda e: e.tensor_tensor(
                        wsum[:, :], wsum[:, :], s2c[:, :], ALU.mult))
                    i2 = kb.ve.op([i1], lambda e: e.tensor_tensor(
                        wsum[:, :], wsum[:, :], s2c[:, :], ALU.mult))
                    i3 = kb.ve.op([i2], lambda e: e.tensor_scalar_max(
                        wsum[:, :], wsum[:, :], 1e-8))
                    i4 = kb.se.op([i3], lambda e: e.activation(wsum[:, :], wsum[:, :], AF.Ln))
                    i5 = kb.se.op([i4], lambda e: e.activation(wsum[:, :], wsum[:, :],
                                                               AF.Exp, scale=-0.5))
                    i6 = kb.ve.op([i5], lambda e: e.tensor_tensor(
                        s2c[:, :], s2c[:, :], wsum[:, :], ALU.mult))
                    a1 = kb.ve.op([i6], lambda e, q_=qt: e.tensor_scalar_mul(
                        Og[:, q_ * DM:(q_ + 1) * DM], Og[:, q_ * DM:(q_ + 1) * DM],
                        s2c[:, :]))

                    tpe = kb.dT[1].start_t(
                        [a1, ep, mpv_by[0], mpv_by[1]],
                        ptT[0][:, 0:DM].rearrange("k (c q) -> k c q", q=128),
                        Og[:, qt * DM:(qt + 1) * DM].rearrange("q (c k) -> q c k", k=128))

                    def f_r(e, q_=qt, gg=g):
                        outap = A.rearrange("p (c t) -> p c t", c=NCH)[
                            :, :, gg * GRP + q_ * 128:gg * GRP + (q_ + 1) * 128]
                        return e.scalar_tensor_tensor(
                            outap, ptT[0][:, 0:DM].rearrange("p (c t) -> p c t", c=NCH), 1.0,
                            outap, ALU.mult, ALU.add)

                    ep = kb.ve.op([tpe], f_r)
                ptT_use = ep
    kb.barrier()
    emit_fm_ln(kb, A, tag="sln3")


def p3_weight_drams(nc):
    return {
        "rotw": nc.dram_tensor("rotw", [3, DM], F32, kind="ExternalInput").ap(),
        "scale": nc.dram_tensor("scale_in", [1, 1], F32, kind="ExternalInput").ap(),
        "bias": nc.dram_tensor("bias_in", [1, 1], F32, kind="ExternalInput").ap(),
    }


def p3_weight_maps(inputs):
    return {
        "rotw": np.asarray(inputs["L1_attn_rot_w"], np.float32),
        "scale_in": np.asarray(inputs["L1_attn_scale"], np.float32).reshape(1, 1),
        "bias_in": np.asarray(inputs["L1_attn_bias"], np.float32).reshape(1, 1),
    }


def build_phase3(dbg=None):
    nc = bass.Bass("TRN2", target_bir_lowering=False)
    cd = declare_const_drams(nc, p1=False)
    s_in = nc.dram_tensor("s_in", [128, NCH * T], BF16, kind="ExternalInput").ap()
    W = p3_weight_drams(nc)
    s_out = nc.dram_tensor("s_out", [128, NCH * T], BF16, kind="ExternalOutput").ap()
    x2i = nc.dram_tensor("x2i", [128, NCH * T], BF16)
    x2g = nc.dram_tensor("x2g", [256, NCH * T], BF16)
    with ExitStack() as ctx:
        kb = KB(nc, ctx)
        A = kb.sb("bufA", [128, NCH * T], BF16)
        load_consts(kb, cd, p1=False)
        kb.dX[1].start([], A[:, :], s_in)
        kb.barrier()
        emit_phase3(kb, A, W, x2i, x2g)
        kb.barrier()
        kb.dS.start([], s_out, A[:, :])
        finish(kb)
    return nc


def emit_phase4(kb, A, W, acc_dram, y_out):
    cons = kb.consts
    FH = 2048
    NQ4 = FH // 256
    TH = T // 2
    NTC = TH // 128
    NTB = TH // 512
    st = kb.scope()
    with st:
        wrg = kb.sb("p4_wrg", [128, NCH * 16], BF16, st)
        rl = kb.sb("p4_rl", [128, 16], F32, st)
        nmr = kb.sb("p4_nmr", [128, 1], F32, st)
        zr = kb.sb("p4_zr", [128, 1], F32, st)
        er = kb.sb("p4_er", [128, 16], F32, st)
        cwc = kb.sb("p4_cwc", [128, NTI * 8], F32, st)
        h = kb.sb("p4_h", [128, 16 * TH], BF16, st)
        wd_sb = kb.sb("p4_wd", [128, 16 * DM], BF16, st)
        wgq = [kb.sb(f"p4_wg{i}", [128, NCH * 256], BF16, st) for i in range(2)]
        wuq = [kb.sb(f"p4_wu{i}", [128, NCH * 256], BF16, st) for i in range(2)]
        gsc = [kb.sb(f"p4_gs{i}", [128, 512], BF16, st) for i in range(2)]
        stage = [kb.sb(f"p4_st{i}", [128, DM], F32, st) for i in range(2)]
        acc_sb = kb.sb("p4_acc", [128, DM], F32, st)
        mucol = kb.sb("p4_mu", [128, 1], F32, st)
        sqcol = kb.sb("p4_sq", [128, 1], F32, st)
        vcol = kb.sb("p4_vc", [128, 1], F32, st)

        t_wrg = kb.dW[5].start([], wrg[:, :].rearrange("p (c f) -> p c f", f=16),
                               W["wrg"].rearrange("(c p) f -> p c f", p=128))
        pst = kb.scope()
        with pst:
            r_ps = kb.ps("p4_rps", [128, 16], F32, pst)
            tp_ps = kb.ps("p4_tp", [128, DM], BF16, pst)
            g4 = [kb.ps(f"p4_g{i}", [128, 512], F32, pst) for i in range(2)]
            u4 = [kb.ps(f"p4_u{i}", [128, 512], F32, pst) for i in range(2)]

            vch = None
            mm_prev = None
            for t in range(NTI):
                def f_r(e, tt=t):
                    last = None
                    for ch in range(NCH):
                        last = e.matmul(r_ps[:, :],
                                        A[:, ch * T + tt * 128:ch * T + (tt + 1) * 128],
                                        wrg[:, ch * 16:(ch + 1) * 16],
                                        start=(ch == 0), stop=(ch == NCH - 1))
                    return last

                mm = kb.pe.op([t_wrg, vch, mm_prev], f_r)
                mm_prev = mm
                vch = kb.ve.op([mm, vch], lambda e: e.tensor_copy(rl[:, :], r_ps[:, :]))
                vch = kb.ve.op([vch], lambda e: e.reduce_max(
                    nmr[:, :], rl[:, 0:8], axis=AX, negate=True))
                s1 = kb.se.op([vch], lambda e: e.activation(
                    er[:, 0:8], rl[:, 0:8], AF.Exp, bias=nmr[:, :],
                    accum_out=zr[:, :]))
                vch = kb.ve.op([s1], lambda e: e.reciprocal(nmr[:, :], zr[:, :]))
                vch = kb.ve.op([vch], lambda e: e.tensor_scalar_mul(
                    er[:, 0:8], er[:, 0:8], nmr[:, :]))
                vch = kb.ve.op([vch], lambda e: e.reduce_max(
                    nmr[:, :], rl[:, 8:16], axis=AX, negate=True))
                vch = kb.ve.op([vch], lambda e: e.tensor_scalar_mul(
                    nmr[:, :], nmr[:, :], 10.0))
                s2 = kb.se.op([vch], lambda e: e.activation(
                    er[:, 8:16], rl[:, 8:16], AF.Exp, scale=10.0, bias=nmr[:, :],
                    accum_out=zr[:, :]))
                vch = kb.ve.op([s2], lambda e: e.reciprocal(nmr[:, :], zr[:, :]))
                vch = kb.ve.op([vch], lambda e: e.tensor_scalar_mul(
                    er[:, 8:16], er[:, 8:16], nmr[:, :]))
                vch = kb.ve.op([vch], lambda e: e.tensor_tensor(
                    er[:, 0:8], er[:, 0:8], er[:, 8:16], ALU.mult))
                vch = kb.ve.op([vch], lambda e: e.reduce_sum(zr[:, :], er[:, 0:8], axis=AX))
                vch = kb.ve.op([vch], lambda e: e.tensor_single_scalar(
                    zr[:, :], zr[:, :], 1e-9, ALU.add))
                vch = kb.ve.op([vch], lambda e: e.reciprocal(zr[:, :], zr[:, :]))
                vch = kb.ve.op([vch], lambda e, tt=t: e.tensor_scalar_mul(
                    cwc[:, tt * 8:(tt + 1) * 8], er[:, 0:8], zr[:, :]))
            cw_tok = vch
            kb.barrier()

            acc_tok = {}
            prev_store = None
            st_use = [None, None]
            wg_tok = [None, None]
            wu_tok = [None, None]
            wg_used = [None, None]
            up_done = [None, None]
            wd_tok = None
            dn_evac = None
            dn_par = [None, None]
            h_used = None
            pre = {}
            slabs = [(hh, ee, ff) for hh in range(2) for ee in range(8)
                     for ff in range(2)]

            def emit_up_load(ei_, fh_, q4_):
                par_ = q4_ % 2
                qg_ = fh_ * NQ4 + q4_
                g_t = []
                u_t = []
                for hv in range(2):
                    sl = slice(hv * 1024, (hv + 1) * 1024)
                    g_t.append(kb.dW[hv].start(
                        [wg_used[par_]],
                        wgq[par_][:, sl].rearrange("p (c f) -> p c f", f=256),
                        W["wg"][ei_][qg_][:, sl].rearrange("p (c f) -> p c f", f=256)))
                    u_t.append(kb.dW[2 + hv].start(
                        [wg_used[par_]],
                        wuq[par_][:, sl].rearrange("p (c f) -> p c f", f=256),
                        W["wu"][ei_][qg_][:, sl].rearrange("p (c f) -> p c f", f=256)))
                return g_t, u_t

            for half in range(2):
                for e_i in range(8):
                    for fh in range(2):
                        hu = h_used
                        wd_tok = kb.dW[4].start(
                            [dn_evac],
                            wd_sb[:, :].rearrange("p (t c) -> p t c", c=DM),
                            W["wd"][e_i][fh].rearrange("p (t c) -> p t c", c=DM))
                        for q4 in range(NQ4):
                            par = q4 % 2
                            key = (half, e_i, fh, q4)
                            if key in pre:
                                wg_tok[par], wu_tok[par] = pre.pop(key)
                            else:
                                wg_tok[par], wu_tok[par] = emit_up_load(e_i, fh, q4)
                            for ftt in range(2):
                                for tb in range(NTB):
                                    gpar = tb % 2

                                    def f_g(e, f_=ftt, b=tb, pp=par, gp=gpar, hh=half):
                                        last = None
                                        for ch in range(NCH):
                                            last = e.matmul(
                                                g4[gp][:, :],
                                                wgq[pp][:, ch * 256 + f_ * 128:ch * 256 + (f_ + 1) * 128],
                                                A[:, ch * T + hh * TH + b * 512:ch * T + hh * TH + (b + 1) * 512],
                                                start=(ch == 0), stop=(ch == NCH - 1))
                                        return last

                                    def f_u(e, f_=ftt, b=tb, pp=par, gp=gpar, hh=half):
                                        last = None
                                        for ch in range(NCH):
                                            last = e.matmul(
                                                u4[gp][:, :],
                                                wuq[pp][:, ch * 256 + f_ * 128:ch * 256 + (f_ + 1) * 128],
                                                A[:, ch * T + hh * TH + b * 512:ch * T + hh * TH + (b + 1) * 512],
                                                start=(ch == 0), stop=(ch == NCH - 1))
                                        return last

                                    mg = kb.pe.op(list(wg_tok[par]) + [cw_tok, up_done[gpar],
                                                   dn_par[0]], f_g)
                                    mu_ = kb.pe.op(list(wu_tok[par]) + [up_done[gpar],
                                                    dn_par[1]], f_u)
                                    wg_used[par] = mu_
                                    sg = kb.se.op([mg, up_done[gpar]],
                                                  lambda e, gp=gpar: e.activation(
                                        gsc[gp][:, :], g4[gp][:, :], AF.Silu))
                                    hd = kb.ve.op([sg, mu_, hu], lambda e, q_=q4, f_=ftt,
                                                  b_=tb, gp=gpar: e.tensor_tensor(
                                        h[:, (q_ * 2 + f_) * TH + b_ * 512:(q_ * 2 + f_) * TH + (b_ + 1) * 512],
                                        gsc[gp][:, :], u4[gp][:, :], ALU.mult))
                                    up_done[gpar] = hd
                        si = slabs.index((half, e_i, fh))
                        if si + 1 < len(slabs):
                            nh, ne, nf = slabs[si + 1]
                            for q4n in range(2):
                                pre[(nh, ne, nf, q4n)] = emit_up_load(ne, nf, q4n)
                        h_full = [up_done[0], up_done[1]]
                        for tc in range(NTC):
                            spar = tc % 2
                            pp4 = g4 if spar == 0 else u4

                            def f_d(e, t_=tc, hh=half, px=pp4):
                                last = None
                                for hf in range(2):
                                    for ft in range(16):
                                        last = e.matmul(
                                            px[hf][:, :],
                                            h[:, ft * TH + t_ * 128:ft * TH + (t_ + 1) * 128],
                                            wd_sb[:, ft * DM + hf * 512:ft * DM + (hf + 1) * 512],
                                            start=(ft == 0), stop=(ft == 15))
                                return last

                            md = kb.pe.op(h_full + [wd_tok, dn_par[spar]], f_d)
                            h_used = md
                            gt = (half, tc)
                            for hf in range(2):
                                dn_par[spar] = kb.ve.op([md, st_use[spar]], lambda e, t_=tc,
                                                        hh=half, sp=spar, ei=e_i, f_=hf,
                                                        px=pp4: e.tensor_scalar_mul(
                                    stage[sp][:, f_ * 512:(f_ + 1) * 512], px[f_][:, :],
                                    cwc[:, (hh * NTC + t_) * 8 + ei:(hh * NTC + t_) * 8 + ei + 1]))
                            dn_evac = dn_par[spar]
                            first = (e_i == 0 and fh == 0)
                            st_use[spar] = kb.dA[spar].start(
                                [dn_evac, acc_tok.get(gt)],
                                acc_dram.ap()[half * TH + tc * 128:half * TH + (tc + 1) * 128, :],
                                stage[spar][:, :],
                                **({} if first else {"accum_op": ALU.add}))
                            acc_tok[gt] = st_use[spar]
                tl = prev_store
                s1_prev = None
                for tc in range(NTC):
                    ld = kb.dX[tc % 2].start([acc_tok[(half, tc)], tl],
                                             acc_sb[:, :],
                                             acc_dram.ap()[half * TH + tc * 128:half * TH + (tc + 1) * 128, :])

                    def f_t(e, t_=tc, hh=half):
                        last = None
                        for c in range(NCH):
                            last = e.transpose(
                                tp_ps[:, c * 128:(c + 1) * 128],
                                A[:, c * T + hh * TH + t_ * 128:c * T + hh * TH + (t_ + 1) * 128],
                                cons["eye"][:, :])
                        return last

                    tpe = kb.pe.op([tl, dn_evac, s1_prev], f_t)
                    s1 = kb.ve.op([ld, tpe, tl], lambda e: e.tensor_tensor(
                        acc_sb[:, :], acc_sb[:, :], tp_ps[:, :], ALU.add))
                    s1_prev = s1
                    s2 = kb.ve.op([s1], lambda e: e.reduce_sum(mucol[:, :], acc_sb[:, :], axis=AX))
                    s3 = kb.se.op([s1, h_used], lambda e: e.activation(
                        h[:, 0:DM], acc_sb[:, :], AF.Square, accum_out=sqcol[:, :]))
                    h_used = s3
                    s4 = kb.ve.op([s2], lambda e: e.tensor_scalar_mul(
                        mucol[:, :], mucol[:, :], 1.0 / DM))
                    s5 = kb.ve.op([s4, s3], lambda e: e.tensor_scalar(
                        sqcol[:, :], sqcol[:, :], 1.0 / DM, EPS, ALU.mult, ALU.add))
                    s6 = kb.ve.op([s5], lambda e: e.tensor_tensor(
                        vcol[:, :], mucol[:, :], mucol[:, :], ALU.mult))
                    s7 = kb.ve.op([s6], lambda e: e.tensor_tensor(
                        vcol[:, :], sqcol[:, :], vcol[:, :], ALU.subtract))
                    s8 = kb.se.op([s7], lambda e: e.activation(vcol[:, :], vcol[:, :], AF.Ln))
                    s9 = kb.se.op([s8], lambda e: e.activation(vcol[:, :], vcol[:, :],
                                                               AF.Exp, scale=-0.5))
                    sa = kb.ve.op([s9, s3], lambda e: e.tensor_scalar(
                        acc_sb[:, :], acc_sb[:, :], mucol[:, :], vcol[:, :],
                        ALU.subtract, ALU.mult))
                    tl = kb.dS.start([sa], y_out[half * TH + tc * 128:half * TH + (tc + 1) * 128, :],
                                     acc_sb[:, :])
                    prev_store = tl
    kb.barrier()


def p4_weight_drams(nc):
    return {
        "wrg": nc.dram_tensor("wrg", [DM, 16], F32, kind="ExternalInput").ap(),
        "wg": nc.dram_tensor("wg4", [8, 16, 128, 2048], F32, kind="ExternalInput").ap(),
        "wu": nc.dram_tensor("wu4", [8, 16, 128, 2048], F32, kind="ExternalInput").ap(),
        "wd": nc.dram_tensor("wd4", [8, 2, 128, 16 * DM], F32, kind="ExternalInput").ap(),
    }


def _p4_reorder_up(w):
    a = np.asarray(w, np.float32).reshape(8, 8, 128, 16, 256)
    return np.ascontiguousarray(a.transpose(0, 3, 2, 1, 4).reshape(8, 16, 128, 2048))


def _p4_reorder_dn(w):
    a = np.asarray(w, np.float32).reshape(8, 2, 16, 128, DM)
    return np.ascontiguousarray(a.transpose(0, 1, 3, 2, 4).reshape(8, 2, 128, 16 * DM))


def p4_weight_maps(inputs):
    return {
        "wrg": np.ascontiguousarray(np.concatenate(
            [np.asarray(inputs["L1_ffn_w_router"], np.float32),
             np.asarray(inputs["L1_ffn_w_gating"], np.float32)], axis=1)),
        "wg4": _p4_reorder_up(inputs["L1_ffn_w_gate"]),
        "wu4": _p4_reorder_up(inputs["L1_ffn_w_up"]),
        "wd4": _p4_reorder_dn(inputs["L1_ffn_w_down"]),
    }


def build_phase4(dbg=None):
    nc = bass.Bass("TRN2", target_bir_lowering=False)
    cd = declare_const_drams(nc, p1=False)
    s_in = nc.dram_tensor("s_in", [128, NCH * T], BF16, kind="ExternalInput").ap()
    W = p4_weight_drams(nc)
    y_out = nc.dram_tensor("y_out", [T, DM], F32, kind="ExternalOutput").ap()
    acc_dram = nc.dram_tensor("acc4", [T, DM], F32)
    with ExitStack() as ctx:
        kb = KB(nc, ctx)
        A = kb.sb("bufA", [128, NCH * T], BF16)
        load_consts(kb, cd, p1=False)
        kb.dX[1].start([], A[:, :], s_in)
        kb.barrier()
        emit_phase4(kb, A, W, acc_dram, y_out)
        finish(kb)
    return nc


def p1_weight_drams(nc):
    W = {}
    for name, shape in [("wqkv", [PAIRS, 128, 3 * DM]),
                        ("wo", [DM, DM]), ("wg1", [DM, 64]), ("wg2", [64, DM]),
                        ("gammabeta", [128, 16])]:
        W[name] = nc.dram_tensor(name, shape, F32, kind="ExternalInput").ap()
    return W


def p1_weight_maps(inputs):
    g = np.asarray(inputs["L0_attn_gamma"], np.float32).reshape(NCH, 128).T
    b = np.asarray(inputs["L0_attn_beta"], np.float32).reshape(NCH, 128).T

    def slab(key):
        w = np.asarray(inputs[key], np.float32).reshape(8, 128, 8, 128)
        return w.transpose(2, 1, 0, 3)

    wqkv = np.stack([slab("L0_attn_Wq"), slab("L0_attn_Wk"), slab("L0_attn_Wv")],
                    axis=2)
    return {
        "wqkv": np.ascontiguousarray(wqkv.reshape(PAIRS, 128, 3 * DM)),
        "wo": np.asarray(inputs["L0_attn_Wo"], np.float32),
        "wg1": np.asarray(inputs["L0_attn_Wg1"], np.float32),
        "wg2": np.asarray(inputs["L0_attn_Wg2"], np.float32),
        "gammabeta": np.ascontiguousarray(np.concatenate([g, b], axis=1)),
    }


def finish(kb, out_tok=None):
    toks = kb.barrier()

    def body(e):
        kb.sy._waits(e, toks)
        e.wait_ge(kb.dS.ctr.h, kb.dS.ctr.n)

    kb.sy._body(body)


def build_phase1(dbg=None):
    nc = bass.Bass("TRN2", target_bir_lowering=False)
    cd = declare_const_drams(nc)
    x0 = nc.dram_tensor("x0", [T, DM], F32, kind="ExternalInput").ap()
    W = p1_weight_drams(nc)
    s_out = nc.dram_tensor("s_out", [128, NCH * T], BF16, kind="ExternalOutput").ap()
    dbg_ap = None
    if dbg:
        dbg_ap = nc.dram_tensor("dbg_o", [128, NCH * T], BF16, kind="ExternalOutput").ap()
    kv_in = [nc.dram_tensor(f"kvi{p}", [64, 128], F32) for p in range(PAIRS)]
    kv_out = [nc.dram_tensor(f"kvo{p}", [128, 128], F32) for p in range(PAIRS)]
    with ExitStack() as ctx:
        kb = KB(nc, ctx)
        A = kb.sb("bufA", [128, NCH * T], BF16)
        B = kb.sb("bufB", [128, NCH * T], BF16)
        load_consts(kb, cd)
        kb.barrier()
        emit_load_x_fm(kb, A, x0)
        emit_phase1(kb, A, B, W, kv_in, kv_out)
        if dbg:
            kb.dS.start([], dbg_ap, B[:, :])
        emit_phase1_tail(kb, A, B, W)
        kb.barrier()
        kb.dS.start([], s_out, A[:, :])
        finish(kb)
    return nc


def fm_to_tm(arr):
    a = np.asarray(arr, BF).astype(np.float32).reshape(128, NCH, T)
    return a.transpose(2, 1, 0).reshape(T, NCH * 128)


def tm_to_fm_f32(x):
    return np.ascontiguousarray(
        x.reshape(T, NCH, 128).transpose(2, 1, 0).reshape(128, NCH * T))


def build_full():
    nc = bass.Bass("TRN2", target_bir_lowering=False)
    cd = declare_const_drams(nc, p1=True)
    x0 = nc.dram_tensor("x0", [T, DM], F32, kind="ExternalInput").ap()
    W1 = p1_weight_drams(nc)
    W2 = p2_weight_drams(nc)
    W3 = p3_weight_drams(nc)
    W4 = p4_weight_drams(nc)
    y_out = nc.dram_tensor("y_out", [T, DM], F32, kind="ExternalOutput").ap()
    kv_in = [nc.dram_tensor(f"kvi{p}", [64, 128], F32) for p in range(PAIRS)]
    kv_out = [nc.dram_tensor(f"kvo{p}", [128, 128], F32) for p in range(PAIRS)]
    x2i = nc.dram_tensor("x2i", [128, NCH * T], BF16)
    x2g = nc.dram_tensor("x2g", [256, NCH * T], BF16)
    acc_dram = nc.dram_tensor("acc4", [T, DM], F32)
    with ExitStack() as ctx:
        kb = KB(nc, ctx)
        A = kb.sb("bufA", [128, NCH * T], BF16)
        load_consts(kb, cd, p1=False)
        kb.barrier()
        emit_load_x_fm(kb, A, x0)
        stB = kb.scope()
        with stB:
            B = kb.sb("bufB", [128, NCH * T], BF16, stB)
            st1c = kb.scope()
            with st1c:
                load_p1_consts(kb, cd, st1c)
                kb.barrier()
                emit_phase1(kb, A, B, W1, kv_in, kv_out)
                emit_phase1_tail(kb, A, B, W1)
            kb.barrier()
            emit_phase2(kb, A, B, W2)
        kb.barrier()
        emit_phase3(kb, A, W3, x2i, x2g)
        emit_phase4(kb, A, W4, acc_dram, y_out)
        finish(kb)
    return nc


def kernel(**inputs):
    x0_full = np.ascontiguousarray(
        np.asarray(inputs["L0_attn_x"], np.float32).reshape(8 * T, DM))
    nc = build_full()
    w = {}
    w.update(p1_weight_maps(inputs))
    w.update(p2_weight_maps(inputs))
    w.update(p3_weight_maps(inputs))
    w.update(p4_weight_maps(inputs))
    maps = []
    for c in range(8):
        m = dict(const_in_maps(c, p1=True))
        m.update(w)
        m["x0"] = np.ascontiguousarray(x0_full[c * T:(c + 1) * T])
        maps.append(m)
    res = run_bass_kernel_spmd(nc, maps, core_ids=list(range(8)))
    y = np.concatenate([np.asarray(res.results[c]["y_out"]) for c in range(8)], axis=0)
    return y.reshape(4, 8192, 1024)


def kernel_unfused(**inputs):
    x0_full = np.ascontiguousarray(
        np.asarray(inputs["L0_attn_x"], np.float32).reshape(8 * T, DM))

    nc1 = build_phase1()
    w1 = p1_weight_maps(inputs)
    maps = []
    for c in range(8):
        m = dict(const_in_maps(c))
        m.update(w1)
        m["x0"] = np.ascontiguousarray(x0_full[c * T:(c + 1) * T])
        maps.append(m)
    r1 = run_bass_kernel_spmd(nc1, maps, core_ids=list(range(8)))
    streams = [np.asarray(r1.results[c]["s_out"]) for c in range(8)]

    nc2 = build_phase2()
    w2 = p2_weight_maps(inputs)
    maps = []
    for c in range(8):
        m = dict(const_in_maps(c, p1=False))
        m.update(w2)
        m["s_in"] = streams[c]
        maps.append(m)
    r2 = run_bass_kernel_spmd(nc2, maps, core_ids=list(range(8)))
    streams = [np.asarray(r2.results[c]["s_out"]) for c in range(8)]

    nc3 = build_phase3()
    w3 = p3_weight_maps(inputs)
    maps = []
    for c in range(8):
        m = dict(const_in_maps(c, p1=False))
        m.update(w3)
        m["s_in"] = streams[c]
        maps.append(m)
    r3 = run_bass_kernel_spmd(nc3, maps, core_ids=list(range(8)))
    streams = [np.asarray(r3.results[c]["s_out"]) for c in range(8)]

    nc4 = build_phase4()
    w4 = p4_weight_maps(inputs)
    maps = []
    for c in range(8):
        m = dict(const_in_maps(c, p1=False))
        m.update(w4)
        m["s_in"] = streams[c]
        maps.append(m)
    r4 = run_bass_kernel_spmd(nc4, maps, core_ids=list(range(8)))
    y = np.concatenate([np.asarray(r4.results[c]["y_out"]) for c in range(8)], axis=0)
    return y.reshape(4, 8192, 1024)


# version 21
# speedup vs baseline: 1.0855x; 1.0855x over previous
from contextlib import ExitStack

import numpy as np
import ml_dtypes

import concourse.bass as bass
import concourse.mybir as mybir
from concourse.bass_utils import run_bass_kernel_spmd

F32 = mybir.dt.float32
BF16 = mybir.dt.bfloat16
U32 = mybir.dt.uint32
AX = mybir.AxisListType.X
ALU = mybir.AluOpType
AF = mybir.ActivationFunctionType
BF = ml_dtypes.bfloat16

DM = 1024
NCH = 8
CHUNK = 128
PAIRS = 8
EPS = 1e-5
RG_PAIRS = [[0, 1], [2, 3], [4, 5], [6, 7]]

T = 4096
SUB = 512


def _derived():
    global NB, NTI, NSC, SBK, NSB, GRP, NGT
    NB = T // 512
    NTI = T // 128
    NSC = T // CHUNK
    SBK = 2 * SUB
    NSB = 2 * T // SBK
    GRP = T // 2
    NGT = GRP // 128


def set_sizes(t=4096, sub=512):
    global T, SUB
    T, SUB = t, sub
    _derived()


_derived()


def _build_slopes(n):
    def po2(m):
        start = 2.0 ** (-2.0 ** (-(np.log2(m) - 3)))
        return [start * start ** i for i in range(m)]
    if np.log2(n).is_integer():
        return po2(n)
    closest = 2 ** int(np.floor(np.log2(n)))
    return po2(closest) + _build_slopes(2 * closest)[0::2][:n - closest]


def host_consts():
    c = {}
    c["eye_bf"] = np.eye(128, dtype=np.float32).astype(BF)
    sw = np.zeros((128, 128), np.float32)
    for p in range(128):
        sw[p, p ^ 1] = 1.0
    c["swap_bf"] = sw.astype(BF)
    c["ones_row_bf"] = np.ones((1, 128), np.float32).astype(BF)
    c["ones_col_bf"] = np.ones((128, 1), np.float32).astype(BF)
    c["ones_col_f32"] = np.ones((128, 1), np.float32)
    c["ones_row_f32"] = np.ones((1, 128), np.float32)

    s = (np.array(_build_slopes(16), dtype=np.float32) * np.float32(1.0 + 1e-05)).astype(np.float32)
    idx = np.arange(CHUNK, dtype=np.float32)
    diff = idx[:, None] - idx[None, :]
    maskT = np.zeros((128, 16 * 128), np.float32)
    for h in range(16):
        m = (np.exp(-s[h] * np.maximum(diff, 0.0)) * (diff >= 0)).astype(np.float32)
        maskT[:, h * 128:(h + 1) * 128] = m.T
    c["maskT_c"] = maskT.astype(BF)
    lam_q = np.exp(-s[:, None] * (idx + 1.0)).astype(np.float32)
    lam_k = np.exp(-s[:, None] * (CHUNK - 1.0 - idx)).astype(np.float32)
    lam_c = np.exp(-s * CHUNK).astype(np.float32)
    lamq = np.zeros((128, 8 * 128), np.float32)
    lamk = np.zeros((128, 8 * 128), np.float32)
    lamc = np.zeros((64, 8 * 128), np.float32)
    for p in range(8):
        lamq[0:64, p * 128:(p + 1) * 128] = lam_q[2 * p][None, :]
        lamq[64:128, p * 128:(p + 1) * 128] = lam_q[2 * p + 1][None, :]
        for col in range(128):
            h = 2 * p + col // 64
            lamk[:, p * 128 + col] = lam_k[h]
            lamc[:, p * 128 + col] = lam_c[h]
    c["lamq_c"] = lamq.astype(BF)
    c["lamk_c"] = lamk.astype(BF)
    c["lamc_c"] = lamc
    c["lamc_pow"] = [[float(np.float32(np.float64(lam_c[h]) ** i)) for h in range(16)]
                     for i in range(64)]
    misc = np.zeros((128, 4), np.float32)
    misc[:, 0] = EPS
    misc[:, 1] = 1e-8
    misc[:, 2] = 1e-9
    c["misc_c"] = misc
    return c


CONSTS = host_consts()

CONST_SPECS = [
    ("c_eye", [128, 128], BF16, "eye_bf"),
    ("c_swap", [128, 128], BF16, "swap_bf"),
    ("c_ones_row", [1, 128], BF16, "ones_row_bf"),
    ("c_ones_col", [128, 1], BF16, "ones_col_bf"),
    ("c_ones_col_f", [128, 1], F32, "ones_col_f32"),
    ("c_ones_row_f", [1, 128], F32, "ones_row_f32"),
    ("c_misc", [128, 4], F32, "misc_c"),
]
P1_CONST_SPECS = [
    ("c_maskT", [128, 16 * 128], BF16, "maskT_c"),
    ("c_lamq", [128, 8 * 128], BF16, "lamq_c"),
    ("c_lamk", [128, 8 * 128], BF16, "lamk_c"),
    ("c_lamc", [64, 8 * 128], F32, "lamc_c"),
]


_NOINC = object()


class Ctr:
    __slots__ = ("h", "n", "name")

    def __init__(self, handle, name):
        self.h = handle
        self.n = 0
        self.name = name


class Eng:

    def __init__(self, kb, name):
        self.kb = kb
        self.name = name
        self.ctr = Ctr(kb.ctx.enter_context(kb.nc.semaphore("c_" + name)), name)
        self.waited = {}
        self.q = []

    def _flush(self):
        if not self.q:
            return
        items = self.q
        self.q = []

        def body(e):
            for deps, f, inc in items:
                self._waits(e, deps)
                instr = f(e)
                if inc:
                    instr.then_inc(self.ctr.h, inc)

        getattr(self.kb.block, self.name)(body)

    def _body(self, f):
        self._flush()
        getattr(self.kb.block, self.name)(f)

    def _waits(self, e, deps):
        need = {}
        for d in deps:
            if d is None:
                continue
            ctr, val = d
            if val is None or val <= 0:
                continue
            if self.waited.get(ctr, 0) >= val:
                continue
            need[ctr] = max(need.get(ctr, 0), val)
        for ctr, val in need.items():
            e.wait_ge(ctr.h, val)
            self.waited[ctr] = val

    def op(self, deps, f, inc=1):
        if inc:
            self.ctr.n += inc
        self.q.append((list(deps), f, inc))
        if len(self.q) >= 256:
            self._flush()
        return (self.ctr, self.ctr.n)

    def tok(self):
        return (self.ctr, self.ctr.n)


class Dma:

    def __init__(self, kb, eng, name):
        self.kb = kb
        self.eng = eng
        self.ctr = Ctr(kb.ctx.enter_context(kb.nc.semaphore("d_" + name)), name)

    def start(self, deps, out, in_, **kw):
        ctr = self.ctr
        ctr.n += 16

        def f(e):
            return e.dma_start(out=out, in_=in_, **kw)

        self.eng.q.append((list(deps), lambda e: f(e).then_inc(ctr.h, 16) or _NOINC, 0))
        if len(self.eng.q) >= 256:
            self.eng._flush()
        return (ctr, ctr.n)

    def start_t(self, deps, out, in_):
        ctr = self.ctr
        ctr.n += 16
        self.eng.q.append((list(deps),
                           lambda e: e.dma_start_transpose(out, in_).then_inc(ctr.h, 16) or _NOINC,
                           0))
        if len(self.eng.q) >= 256:
            self.eng._flush()
        return (ctr, ctr.n)


class KBScope(ExitStack):

    def __init__(self, kb):
        super().__init__()
        self.kb = kb

    def __exit__(self, *a):
        for e in self.kb.engs:
            e._flush()
        return super().__exit__(*a)


class KB:
    def __init__(self, nc, ctx):
        self.nc = nc
        self.ctx = ctx
        self.block = ctx.enter_context(nc.Block())
        self.pe = Eng(self, "tensor")
        self.ve = Eng(self, "vector")
        self.se = Eng(self, "scalar")
        self.pl = Eng(self, "gpsimd")
        self.sy = Eng(self, "sync")
        self.engs = [self.pe, self.ve, self.se, self.pl, self.sy]
        self.dW = [Dma(self, self.pl, f"w{i}") for i in range(6)]
        self.dX = [Dma(self, self.sy, f"x{i}") for i in range(2)]
        self.dS = Dma(self, self.sy, "st")
        self.dT = [Dma(self, self.sy, f"tp{i}") for i in range(2)]
        self.dA = [Dma(self, self.pl, f"ac{i}") for i in range(2)]
        self.cc = Ctr(ctx.enter_context(nc.semaphore("cc")), "cc")
        self.consts = {}

    def sb(self, name, shape, dt, stack=None):
        return (stack or self.ctx).enter_context(self.nc.sbuf_tensor(name, shape, dt))

    def ps(self, name, shape, dt, stack=None):
        return (stack or self.ctx).enter_context(self.nc.psum_tensor(name, shape, dt))

    def collective(self, deps, kind, op, ins, outs):
        cc = self.cc
        cc.n += 1
        self.pl.q.append((list(deps),
                          lambda e: e.collective_compute(
                              kind, op, replica_groups=RG_PAIRS,
                              ins=ins, outs=outs).then_inc(cc.h) or _NOINC, 0))
        return (cc, cc.n)

    def scope(self):
        return KBScope(self)

    def barrier(self):
        for e in self.engs:
            e._flush()
        toks = [(e.ctr, e.ctr.n) for e in self.engs]
        toks += [(d.ctr, d.ctr.n) for d in self.dW + self.dX + self.dA + self.dT + [self.dS]]
        toks.append((self.cc, self.cc.n))
        for e in self.engs:
            def body(en, ee=e):
                ee._waits(en, toks)
            e._body(body)
        return toks


def declare_const_drams(nc, p1=True):
    cd = {}
    specs = CONST_SPECS + (P1_CONST_SPECS if p1 else [])
    for name, shape, dt, _ in specs:
        cd[name] = nc.dram_tensor(name, shape, dt, kind="ExternalInput").ap()
    if p1:
        cd["c_m0"] = nc.dram_tensor("c_m0", [64, 1], F32, kind="ExternalInput").ap()
    return cd


def const_in_maps(core_id, p1=True):
    m = {}
    specs = CONST_SPECS + (P1_CONST_SPECS if p1 else [])
    for name, shape, dt, key in specs:
        m[name] = CONSTS[key]
    if p1:
        m["c_m0"] = np.full((64, 1), 1.0 if core_id % 2 == 1 else 0.0, np.float32)
    return m


_KEYMAP = {"c_eye": "eye", "c_swap": "swap", "c_ones_row": "ones_row",
           "c_ones_col": "ones_col", "c_ones_col_f": "ones_col_f",
           "c_ones_row_f": "ones_row_f", "c_maskT": "maskT", "c_lamq": "lamq",
           "c_misc": "misc", "c_lamk": "lamk", "c_lamc": "lamc", "c_m0": "m0"}


def load_consts(kb, cd, p1=True, stack=None):
    tok = None
    for name, shape, dt, _ in CONST_SPECS:
        tile = kb.sb("sb" + name, shape, dt)
        tok = kb.dX[0].start([], tile[:, :], cd[name])
        kb.consts[_KEYMAP[name]] = tile
    if p1:
        tok = load_p1_consts(kb, cd, stack)
    kb.consts["_tok"] = tok
    return tok


def load_p1_consts(kb, cd, stack=None):
    tok = None
    for name, shape, dt, _ in P1_CONST_SPECS:
        tile = kb.sb("sb" + name, shape, dt, stack)
        tok = kb.dX[0].start([], tile[:, :], cd[name])
        kb.consts[_KEYMAP[name]] = tile
    tile = kb.sb("sbc_m0", [64, 1], F32, stack)
    tok = kb.dX[0].start([], tile[:, :], cd["c_m0"])
    kb.consts["m0"] = tile
    return tok


def emit_load_x_fm(kb, A, x_dram):
    st = kb.scope()
    with st:
        xtm = [kb.sb(f"lx_tm{i}", [128, DM], BF16, st) for i in range(2)]
        ld_tok = [None, None]
        tp_tok = [None, None]
        for t in range(NTI):
            par = t % 2
            ld_tok[par] = kb.dW[par].start([tp_tok[par]], xtm[par][:, :],
                                           x_dram[t * 128:(t + 1) * 128, :])
            outap = A.rearrange("p (c t) -> p c t", c=NCH)[:, :, t * 128:(t + 1) * 128]
            tp_tok[par] = kb.dT[par].start_t(
                [ld_tok[par]], outap,
                xtm[par][:, :].rearrange("q (c k) -> q c k", k=128))
    kb.barrier()


def emit_fm_ln(kb, X, gamma_col=None, beta_col=None, out=None, tag="ln"):
    out = out if out is not None else X
    nb = T // 512
    st = kb.scope()
    with st:
        sq = [kb.sb(f"{tag}_sq{i}", [128, 512], BF16, st) for i in range(2)]
        ssum = kb.sb(f"{tag}_su", [1, 512], F32, st)
        ssq = kb.sb(f"{tag}_qq", [1, 512], F32, st)
        rstd = kb.sb(f"{tag}_rs", [1, 512], F32, st)
        nmu = kb.sb(f"{tag}_nm", [1, 512], F32, st)
        tmp = [kb.sb(f"{tag}_tp{i}", [128, 512], F32, st) for i in range(2)]
        tmp2 = kb.sb(f"{tag}_tpp", [128, 512], F32, st)
        rs_sb = [kb.sb(f"{tag}_rsb{i}", [128, 512], F32, st) for i in range(2)]
        nm_sb = [kb.sb(f"{tag}_nsb{i}", [128, 512], F32, st) for i in range(2)]
        sum_ps = kb.ps(f"{tag}_sps", [1, 512], F32, st)
        sq_ps = kb.ps(f"{tag}_qps", [1, 512], F32, st)
        rs_ps = [kb.ps(f"{tag}_rps{i}", [128, 512], F32, st) for i in range(2)]
        nm_ps = [kb.ps(f"{tag}_nps{i}", [128, 512], F32, st) for i in range(2)]

        ev_sum = ev_sq = None
        sq_w = [None, None]
        ap_w = [None, None]
        row_use = None
        for blk in range(nb):
            def f_sum(e, b=blk):
                last = None
                for ch in range(NCH):
                    last = e.matmul(sum_ps[:, :], kb.consts["ones_col"][:, :],
                                    X[:, ch * T + b * 512:ch * T + (b + 1) * 512],
                                    start=(ch == 0), stop=(ch == NCH - 1))
                return last

            mm_sum = kb.pe.op([ev_sum], f_sum)
            ev_sum = kb.ve.op([mm_sum, row_use], lambda e: e.tensor_copy(
                ssum[:, :], sum_ps[:, :]))
            mm_sq = None
            for ch in range(NCH):
                par = ch % 2
                a_tok = kb.se.op([sq_w[par]], lambda e, c=ch, b=blk, pp=par: e.activation(
                    sq[pp][:, :], X[:, c * T + b * 512:c * T + (b + 1) * 512], AF.Square))
                mm_sq = kb.pe.op([a_tok, ev_sq], lambda e, c=ch, pp=par: e.matmul(
                    sq_ps[:, :], kb.consts["ones_col"][:, :], sq[pp][:, :],
                    start=(c == 0), stop=(c == NCH - 1)))
                sq_w[par] = mm_sq
            ev_sq = kb.ve.op([mm_sq, row_use], lambda e: e.tensor_copy(
                ssq[:, :], sq_ps[:, :]))
            t_mu = kb.ve.op([ev_sum], lambda e: e.tensor_scalar_mul(
                ssum[:, :], ssum[:, :], 1.0 / DM))
            t_m2 = kb.ve.op([t_mu], lambda e: e.tensor_tensor(
                nmu[:, :], ssum[:, :], ssum[:, :], ALU.mult))
            t_v1 = kb.ve.op([ev_sq], lambda e: e.tensor_scalar(
                ssq[:, :], ssq[:, :], 1.0 / DM, EPS, ALU.mult, ALU.add))
            t_v2 = kb.ve.op([t_v1, t_m2], lambda e: e.tensor_tensor(
                rstd[:, :], ssq[:, :], nmu[:, :], ALU.subtract))
            t_ln = kb.se.op([t_v2], lambda e: e.activation(rstd[:, :], rstd[:, :], AF.Ln))
            t_rs = kb.se.op([t_ln], lambda e: e.activation(rstd[:, :], rstd[:, :],
                                                           AF.Exp, scale=-0.5))
            t_nmu = kb.ve.op([t_rs, t_mu], lambda e: e.scalar_tensor_tensor(
                nmu[:, :], ssum[:, :], -1.0, rstd[:, :], ALU.mult, ALU.mult))
            par = blk % 2
            bc1 = kb.pe.op([t_rs, t_nmu] + (ap_w[par] or []), lambda e, pp=par: e.matmul(
                rs_ps[pp][:, :], kb.consts["ones_row_f"][:, :], rstd[0:1, :],
                start=True, stop=True))
            bc2 = kb.pe.op([t_rs, t_nmu] + (ap_w[par] or []), lambda e, pp=par: e.matmul(
                nm_ps[pp][:, :], kb.consts["ones_row_f"][:, :], nmu[0:1, :],
                start=True, stop=True))
            row_use = bc2
            cp_rs = kb.se.op([bc1] + (ap_w[par] or []), lambda e, pp=par: e.activation(
                rs_sb[pp][:, :], rs_ps[pp][:, :], AF.Copy))
            cp_nm = kb.se.op([bc2] + (ap_w[par] or []), lambda e, pp=par: e.activation(
                nm_sb[pp][:, :], nm_ps[pp][:, :], AF.Copy))
            last_v = last_p = None
            for ch in range(NCH):
                on_ve = (ch % 2 == 0)
                eng = kb.ve if on_ve else kb.pl
                tb_ = tmp[par] if on_ve else tmp2
                lst = last_v if on_ve else last_p
                rs_src = rs_ps[par] if on_ve else rs_sb[par]
                nm_src = nm_ps[par] if on_ve else nm_sb[par]

                def f_m(e, c=ch, b=blk, tb=tb_, rr=rs_src):
                    sl = slice(c * T + b * 512, c * T + (b + 1) * 512)
                    return e.tensor_tensor(tb[:, :], X[:, sl], rr[:, :], ALU.mult)

                t1 = eng.op([bc1, bc2, cp_rs, cp_nm, lst], f_m)
                if gamma_col is None:
                    def f_a(e, c=ch, b=blk, tb=tb_, nn=nm_src):
                        sl = slice(c * T + b * 512, c * T + (b + 1) * 512)
                        return e.tensor_tensor(out[:, sl], tb[:, :], nn[:, :], ALU.add)
                    res = eng.op([t1], f_a)
                else:
                    t2 = eng.op([t1], lambda e, tb=tb_, nn=nm_src: e.tensor_tensor(
                        tb[:, :], tb[:, :], nn[:, :], ALU.add))
                    def f_a(e, c=ch, b=blk, tb=tb_):
                        sl = slice(c * T + b * 512, c * T + (b + 1) * 512)
                        return e.tensor_scalar(out[:, sl], tb[:, :],
                                               gamma_col[:, c:c + 1], beta_col[:, c:c + 1],
                                               ALU.mult, ALU.add)
                    res = eng.op([t2], f_a)
                if on_ve:
                    last_v = res
                else:
                    last_p = res
            ap_w[par] = [last_v, last_p]
    kb.barrier()


def emit_phase1(kb, A, B, W, kv_in, kv_out):
    cons = kb.consts
    lamc_pow = CONSTS["lamc_pow"]

    st = kb.scope()
    with st:
        qf = kb.sb("p1_qf", [128, T], BF16, st)
        kf = kb.sb("p1_kf", [128, T], BF16, st)
        vf = kb.sb("p1_vf", [128, 2 * 512], BF16, st)
        ktm = kb.sb("p1_ktm", [128, T], BF16, st)
        vtm = kb.sb("p1_vtm", [128, T], BF16, st)
        qsc0 = kb.sb("p1_qsc0", [64, 128], BF16, st)
        qsc1 = kb.sb("p1_qsc1", [64, 128], BF16, st)
        sm0 = kb.sb("p1_sm0", [128, 128], BF16, st)
        sm1 = kb.sb("p1_sm1", [128, 128], BF16, st)
        q20 = kb.sb("p1_q20", [64, 128], BF16, st)
        q21 = kb.sb("p1_q21", [64, 128], BF16, st)
        kv = kb.sb("p1_kv", [64, 128], F32, st)
        kv3 = kb.sb("p1_kv3", [64, 128], F32, st)
        d_all = kb.sb("p1_dall", [64, NSC * 128], BF16, st)
        kvb_all = kb.sb("p1_kvball", [64, NSC * 128], BF16, st)
        kvt = kb.sb("p1_kvt", [64, 128], F32, st)
        kvall = kb.sb("p1_kvall", [64, 8 * 128], F32, st)
        kvp0 = kb.sb("p1_kvp0", [64, 128], F32, st)
        kvp1 = kb.sb("p1_kvp1", [64, 128], F32, st)
        kv32b = kb.sb("p1_kv32b", [64, 8 * 128], BF16, st)
        wbuf = kb.sb("p1_wbuf", [128, 3 * DM], BF16, st)
        wbuf_used = [None]

        pst = kb.scope()
        with pst:
            qkv_ps = [kb.ps(f"p1_qkv{i}", [128, 512], F32, pst) for i in range(2)]
            s_ps = [kb.ps(f"p1_s{i}", [128, 128], F32, pst) for i in range(2)]
            o_ps2 = [kb.ps(f"p1_o{i}", [64, 256], F32, pst) for i in range(2)]
            d_ps = [kb.ps(f"p1_d{i}", [64, 128], F32, pst) for i in range(2)]

            corr_done = None
            prev_m2 = None
            pend = None
            pool_cp_last = None

            def emit_corr(pd):
                pp = pd["p"]
                c_oe = [pd["oe0"], pd["oe1"]]
                co_by = [pd["oe0"], pd["oe1"]]
                cq_last = pd["qe"]
                for i in range(NSC):
                    cq0 = kb.ve.op([pd["qe"], c_oe[0]], lambda e, ii=i, q_=pp:
                                   e.scalar_tensor_tensor(q20[:, :], qf[0:64, ii * 128:(ii + 1) * 128],
                                                          lamc_pow[ii][2 * q_],
                                                          cons["lamq"][0:64, q_ * 128:(q_ + 1) * 128],
                                                          ALU.mult, ALU.mult))
                    cq1 = kb.ve.op([pd["qe"], c_oe[1]], lambda e, ii=i, q_=pp:
                                   e.scalar_tensor_tensor(q21[:, :], qf[64:128, ii * 128:(ii + 1) * 128],
                                                          lamc_pow[ii][2 * q_ + 1],
                                                          cons["lamq"][64:128, q_ * 128:(q_ + 1) * 128],
                                                          ALU.mult, ALU.mult))
                    cpar = i % 2
                    co = kb.pe.op([pd["m2"], cq0, cq1, c_oe[0], c_oe[1], pd["o_tok"],
                                   co_by[cpar]],
                                  lambda e, q_=pp, cp_=cpar: (
                        e.matmul(o_ps2[cp_][:, 0:128], kv32b[:, q_ * 128:q_ * 128 + 64], q20[:, :],
                                 start=True, stop=False),
                        e.matmul(o_ps2[cp_][:, 128:256], kv32b[:, q_ * 128 + 64:q_ * 128 + 128],
                                 q21[:, :], start=False, stop=True))[-1])
                    c_oe[0] = kb.ve.op([co], lambda e, ii=i, q_=pp, cp_=cpar: e.tensor_tensor(
                        B[0:64, q_ * T + ii * 128:q_ * T + (ii + 1) * 128], o_ps2[cp_][:, 0:128],
                        B[0:64, q_ * T + ii * 128:q_ * T + (ii + 1) * 128], ALU.add))
                    c_oe[1] = kb.ve.op([co], lambda e, ii=i, q_=pp, cp_=cpar: e.tensor_tensor(
                        B[64:128, q_ * T + ii * 128:q_ * T + (ii + 1) * 128], o_ps2[cp_][:, 128:256],
                        B[64:128, q_ * T + ii * 128:q_ * T + (ii + 1) * 128], ALU.add))
                    co_by[cpar] = c_oe[1]
                    cq_last = cq1
                return c_oe[0], c_oe[1], cq_last
            sm_tok = [None, None]
            o_tok = None
            oe_tok = [None, None]
            dused_tok = None
            for p in range(PAIRS):
                wb = wbuf
                wt0 = kb.dW[0].start(
                    [wbuf_used[0]],
                    wbuf[:, 0:1536].rearrange("p (g f) -> p g f", f=512),
                    W["wqkv"][p][:, 0:1536].rearrange("p (g f) -> p g f", f=512))
                wt1 = kb.dW[1].start(
                    [wbuf_used[0]],
                    wbuf[:, 1536:3072].rearrange("p (g f) -> p g f", f=512),
                    W["wqkv"][p][:, 1536:3072].rearrange("p (g f) -> p g f", f=512))
                wtoks = [wt0, wt1, wt1]
                wtoks_k = [wt0, wt1]
                pend_o = [pend["o_tok"], pend["oe0"], pend["oe1"]] if pend else []
                ev = {"q": [], "k": [], "v": []}
                se_prev = [None, None]
                tm_toks = []
                ks_toks = []
                vtp_toks = {}
                ev_prev = None

                def emit_qkv_mm(wi, wkey, blk, extra_deps=()):
                    gi = wi * NB + blk
                    par = gi % 2

                    def f_mm(e, w=wi, b=blk, pp=par, w_=wb):
                        last = None
                        for ch in range(NCH):
                            last = e.matmul(
                                qkv_ps[pp][:, :],
                                w_[:, w * DM + ch * 128:w * DM + (ch + 1) * 128],
                                A[:, ch * T + b * 512:ch * T + (b + 1) * 512],
                                start=(ch == 0), stop=(ch == NCH - 1))
                        return last

                    deps = wtoks_k if wkey == "k" else [wtoks[wi]]
                    mm = kb.pe.op(deps + [se_prev[par]] + list(extra_deps), f_mm)
                    wbuf_used[0] = mm
                    if wkey == "v":
                        a_tok = kb.se.op([mm], lambda e, b=blk, pp=par: e.activation(
                            vf[:, (b % 2) * 512:(b % 2 + 1) * 512], qkv_ps[pp][:, :], AF.Silu))
                    else:
                        d = qf if wkey == "q" else kf
                        a_tok = kb.se.op([mm], lambda e, dd=d, b=blk, pp=par: e.activation(
                            dd[:, b * 512:(b + 1) * 512], qkv_ps[pp][:, :], AF.Silu))
                    se_prev[par] = a_tok
                    ev[wkey].append(a_tok)

                for blk in range(NB):
                    emit_qkv_mm(1, "k", blk, pend_o if blk == 0 else ())
                for blk in range(NB):
                    extra = [vtp_toks[blk - 2]] if blk >= 2 else list(pend_o)
                    emit_qkv_mm(2, "v", blk, extra)
                    tk = kb.dT[0].start_t(
                        [ev["k"][blk], ks_toks[-1] if ks_toks else None],
                        ktm[:, blk * 512:(blk + 1) * 512].rearrange("k (c d) -> k c d", d=128),
                        kf[:, blk * 512:(blk + 1) * 512].rearrange("d (c k) -> d c k", k=128))
                    tv = kb.dT[1].start_t(
                        [ev["v"][blk], tm_toks[-1] if tm_toks else None],
                        vtm[:, blk * 512:(blk + 1) * 512].rearrange("k (c d) -> k c d", d=128),
                        vf[:, (blk % 2) * 512:(blk % 2) * 512 + 512].rearrange("d (c k) -> d c k", k=128))
                    vtp_toks[blk] = tv
                    tm_toks.append(tv)
                    kst = kb.ve.op([tk, tv], lambda e, b=blk, pp=p: e.tensor_tensor(
                        ktm[:, b * 512:(b + 1) * 512].rearrange("p (a f) -> p a f", f=128),
                        ktm[:, b * 512:(b + 1) * 512].rearrange("p (a f) -> p a f", f=128),
                        cons["lamk"][:, pp * 128:(pp + 1) * 128].unsqueeze(1).broadcast_to([128, 4, 128]),
                        ALU.mult))
                    ks_toks.append(kst)
                corr_q_last = None
                if pend is not None:
                    a0, a1_, corr_q_last = emit_corr(pend)
                    oe_tok = [a0, a1_]
                    corr_done = a1_
                    pend = None
                for blk in range(NB):
                    emit_qkv_mm(0, "q", blk, [corr_q_last] if corr_q_last else ())
                d_toks = []
                dev_tok = [None, None]
                for i in range(NSC):
                    dpar = i % 2
                    d_tok = kb.pe.op([ks_toks[-1], tm_toks[-1], dev_tok[dpar], o_tok],
                                     lambda e, ii=i, dp=dpar: (
                        e.matmul(d_ps[dp][:, 0:64], ktm[:, ii * 128:ii * 128 + 64],
                                 vtm[:, ii * 128:ii * 128 + 64], start=True, stop=False),
                        e.matmul(d_ps[dp][:, 64:128], ktm[:, ii * 128 + 64:(ii + 1) * 128],
                                 vtm[:, ii * 128 + 64:(ii + 1) * 128], start=False, stop=True))[-1])
                    dev_tok[dpar] = kb.se.op([d_tok], lambda e, ii=i, dp=dpar: e.activation(
                        d_all[:, ii * 128:(ii + 1) * 128], d_ps[dp][:, :], AF.Copy))
                    d_toks.append(dev_tok[dpar])
                kvb_toks = []
                buf3 = [kv, kvt, kv3]
                z0 = kb.ve.op([corr_done, o_tok, pool_cp_last],
                              lambda e: e.memset(kv[:, :], 0.0))
                kv_tok = z0
                m2 = None
                oe_by = [oe_tok[0], oe_tok[1]]
                sm_tok = [sm_tok[0], sm_tok[1]]
                for i in range(NSC):
                    cur = buf3[i % 3]
                    nxt = buf3[(i + 1) % 3]
                    kvb_toks.append(kb.pl.op([kv_tok], lambda e, ii=i, cc=cur:
                                             e.tensor_copy(
                        kvb_all[:, ii * 128:(ii + 1) * 128], cc[:, :])))
                    m_t = kb.ve.op([kv_tok, kvb_toks[i - 2] if i >= 2 else None],
                                   lambda e, pp=p, cc=cur, nn=nxt: e.tensor_tensor(
                        nn[:, :], cc[:, :], cons["lamc"][:, pp * 128:(pp + 1) * 128],
                        ALU.mult))
                    kv_tok = kb.ve.op([m_t, d_toks[i]], lambda e, ii=i, nn=nxt:
                                      e.tensor_tensor(
                        nn[:, :], nn[:, :], d_all[:, ii * 128:(ii + 1) * 128], ALU.add))
                    if i == NSC - 1:
                        pool_cp_last = kvb_toks[-1]
                        cp = kb.ve.op([kv_tok], lambda e, pp=p, ff=buf3[NSC % 3]:
                                      e.tensor_copy(
                            kvall[:, pp * 128:(pp + 1) * 128], ff[:, :]))
                        d1 = kb.dX[1].start([cp], kv_in[p].ap()[:, :],
                                            kvall[:, p * 128:(p + 1) * 128])
                        cct = kb.collective([d1], "AllGather", ALU.bypass,
                                            [kv_in[p].ap().opt()],
                                            [kv_out[p].ap().opt()])
                    s0 = kb.pe.op([ev["k"][-1], ev["q"][-1], sm_tok[0]],
                                  lambda e, ii=i: e.matmul(
                        s_ps[0][:, :], kf[0:64, ii * 128:(ii + 1) * 128],
                        qf[0:64, ii * 128:(ii + 1) * 128], start=True, stop=True))
                    s1 = kb.pe.op([ev["k"][-1], ev["q"][-1], sm_tok[1]],
                                  lambda e, ii=i: e.matmul(
                        s_ps[1][:, :], kf[64:128, ii * 128:(ii + 1) * 128],
                        qf[64:128, ii * 128:(ii + 1) * 128], start=True, stop=True))
                    sm_tok[0] = kb.ve.op([s0, o_tok], lambda e, pp=p: e.tensor_tensor(
                        sm0[:, :], s_ps[0][:, :],
                        cons["maskT"][:, (2 * pp) * 128:(2 * pp + 1) * 128], ALU.mult))
                    sm_tok[1] = kb.ve.op([s1, o_tok], lambda e, pp=p: e.tensor_tensor(
                        sm1[:, :], s_ps[1][:, :],
                        cons["maskT"][:, (2 * pp + 1) * 128:(2 * pp + 2) * 128], ALU.mult))
                    qt0 = kb.ve.op([ev["q"][-1], o_tok], lambda e, ii=i, pp=p: e.tensor_tensor(
                        qsc0[:, :], qf[0:64, ii * 128:(ii + 1) * 128],
                        cons["lamq"][0:64, pp * 128:(pp + 1) * 128], ALU.mult))
                    qt1 = kb.ve.op([ev["q"][-1], o_tok], lambda e, ii=i, pp=p: e.tensor_tensor(
                        qsc1[:, :], qf[64:128, ii * 128:(ii + 1) * 128],
                        cons["lamq"][64:128, pp * 128:(pp + 1) * 128], ALU.mult))
                    opar = i % 2
                    o_tok = kb.pe.op(
                        [sm_tok[0], sm_tok[1], kvb_toks[i], qt0, qt1,
                         oe_by[opar]],
                        lambda e, ii=i, op_=opar: (
                            e.matmul(o_ps2[op_][:, 0:128], vtm[:, ii * 128:ii * 128 + 64],
                                     sm0[:, :], start=True, stop=False),
                            e.matmul(o_ps2[op_][:, 0:128], kvb_all[:, ii * 128:ii * 128 + 64],
                                     qsc0[:, :], start=False, stop=False),
                            e.matmul(o_ps2[op_][:, 128:256], vtm[:, ii * 128 + 64:(ii + 1) * 128],
                                     sm1[:, :], start=False, stop=False),
                            e.matmul(o_ps2[op_][:, 128:256], kvb_all[:, ii * 128 + 64:(ii + 1) * 128],
                                     qsc1[:, :], start=False, stop=True))[-1])
                    oe_tok[0] = kb.se.op([o_tok], lambda e, ii=i, pp=p, op_=opar: e.activation(
                        B[0:64, pp * T + ii * 128:pp * T + (ii + 1) * 128],
                        o_ps2[op_][:, 0:128], AF.Copy))
                    oe_tok[1] = kb.se.op([o_tok], lambda e, ii=i, pp=p, op_=opar: e.activation(
                        B[64:128, pp * T + ii * 128:pp * T + (ii + 1) * 128],
                        o_ps2[op_][:, 128:256], AF.Copy))
                    oe_by[opar] = oe_tok[1]
                l0 = kb.dX[1].start([cct, prev_m2], kvp0[:, :],
                                    kv_out[p].ap()[0:64, :])
                l1 = kb.dX[1].start([cct, prev_m2], kvp1[:, :],
                                    kv_out[p].ap()[64:128, :])
                m1 = kb.ve.op([l0, l1], lambda e: e.tensor_scalar_mul(
                    kvp0[:, :], kvp0[:, :], cons["m0"][:, :]))
                m2 = kb.ve.op([m1], lambda e, pp=p: e.tensor_copy(
                    kv32b[:, pp * 128:(pp + 1) * 128], kvp0[:, :]))
                prev_m2 = m2
                pend = {"p": p, "m2": m2, "qe": ev["q"][-1],
                        "o_tok": o_tok, "oe0": oe_tok[0], "oe1": oe_tok[1]}
            emit_corr(pend)
    kb.barrier()


def emit_phase1_tail(kb, A, B, W):
    cons = kb.consts
    st = kb.scope()
    with st:
        gb = kb.sb("p1_gb", [128, 16], F32, st)
        t_g = kb.dX[0].start([], gb[:, :], W["gammabeta"])
        kb.barrier()
        emit_fm_ln(kb, B, gamma_col=gb[:, 0:8], beta_col=gb[:, 8:16], tag="oln")

        wo_sb = kb.sb("p1_wo", [128, NCH * DM], BF16, st)
        wg1_sb = kb.sb("p1_wg1", [128, NCH * 64], BF16, st)
        wg2_sb = kb.sb("p1_wg2", [64, DM], BF16, st)
        g1t = kb.sb("p1_g1t", [64, T], BF16, st)
        gt = [kb.sb(f"p1_gt{i}", [128, 512], BF16, st) for i in range(2)]
        og = kb.sb("p1_og", [128, NCH * 512], BF16, st)
        t_wo = kb.dW[0].start([], wo_sb[:, :].rearrange("p (c f) -> p c f", c=NCH),
                              W["wo"].rearrange("(c p) f -> p c f", p=128))
        t_w1 = kb.dW[1].start([], wg1_sb[:, :].rearrange("p (c f) -> p c f", c=NCH),
                              W["wg1"].rearrange("(c p) f -> p c f", p=128))
        t_w2 = kb.dW[2].start([], wg2_sb[:, :], W["wg2"])
        pst = kb.scope()
        with pst:
            g1_ps = kb.ps("p1_g1ps", [64, 512], F32, pst)
            g2_ps = [kb.ps(f"p1_g2ps{i}", [128, 512], F32, pst) for i in range(2)]
            at_ps = [kb.ps(f"p1_atps{i}", [128, 512], F32, pst) for i in range(2)]
            ev_g1 = None
            gt_used = [None, None]
            res_tok = [None, None]
            for blk in range(NB):
                def f_g1(e, b=blk):
                    last = None
                    for ch in range(NCH):
                        last = e.matmul(g1_ps[:, :],
                                        wg1_sb[:, ch * 64:(ch + 1) * 64],
                                        A[:, ch * T + b * 512:ch * T + (b + 1) * 512],
                                        start=(ch == 0), stop=(ch == NCH - 1))
                    return last

                mm_g1 = kb.pe.op([t_w1, ev_g1], f_g1)
                ev_g1 = kb.se.op([mm_g1], lambda e, b=blk: e.activation(
                    g1t[:, b * 512:(b + 1) * 512], g1_ps[:, :], AF.Copy))
                og_toks = []
                for co in range(NCH):
                    par = co % 2
                    mm_g2 = kb.pe.op([t_w2, ev_g1, gt_used[par]], lambda e, c=co, b=blk, pp=par: e.matmul(
                        g2_ps[pp][:, :], wg2_sb[:, c * 128:(c + 1) * 128],
                        g1t[:, b * 512:(b + 1) * 512], start=True, stop=True))
                    sg = kb.se.op([mm_g2], lambda e, pp=par: e.activation(
                        gt[pp][:, :], g2_ps[pp][:, :], AF.Sigmoid))
                    ogt = kb.ve.op([sg], lambda e, c=co, b=blk, pp=par: e.tensor_tensor(
                        og[:, c * 512:(c + 1) * 512], B[:, c * T + b * 512:c * T + (b + 1) * 512],
                        gt[pp][:, :], ALU.mult))
                    gt_used[par] = ogt
                    og_toks.append(ogt)
                for co in range(NCH):
                    par = co % 2

                    def f_at(e, c=co, pp=par):
                        last = None
                        for ci in range(NCH):
                            last = e.matmul(at_ps[pp][:, :],
                                            wo_sb[:, ci * DM + c * 128:ci * DM + (c + 1) * 128],
                                            og[:, ci * 512:(ci + 1) * 512],
                                            start=(ci == 0), stop=(ci == NCH - 1))
                        return last

                    mm_at = kb.pe.op([t_wo] + og_toks + [res_tok[par]], f_at)
                    res_tok[par] = kb.ve.op([mm_at], lambda e, c=co, b=blk, pp=par:
                                            e.scalar_tensor_tensor(
                        B[:, c * T + b * 512:c * T + (b + 1) * 512], at_ps[pp][:, :], 1.0,
                        A[:, c * T + b * 512:c * T + (b + 1) * 512], ALU.mult, ALU.add))
        kb.barrier()
        emit_fm_ln(kb, B, out=A, tag="sln1")


def emit_phase2(kb, A, B, W):
    cons = kb.consts
    st = kb.scope()
    with st:
        wgr = kb.sb("p2_wgr", [128, NCH * DM], BF16, st)
        bgr = kb.sb("p2_bgr", [1, DM], BF16, st)
        ber = kb.sb("p2_ber", [1, 8 * DM], BF16, st)
        web = [kb.sb(f"p2_we{i}", [128, NCH * DM], BF16, st) for i in range(2)]
        lgf = kb.sb("p2_lgf", [128, DM], F32, st)
        zcol = kb.sb("p2_zcol", [128, 1], F32, st)
        pcol = kb.sb("p2_pcol", [128, 1], F32, st)
        f8 = kb.sb("p2_f8", [128, 8], F32, st)
        u8 = kb.sb("p2_u8", [128, 8], U32, st)
        esel = kb.sb("p2_esel", [128, 1], U32, st)
        wcols = kb.sb("p2_wcols", [128, NTI * 8], F32, st)
        rsc = [kb.sb(f"p2_rsc{i}", [128, DM], BF16, st) for i in range(2)]
        accT0 = kb.sb("p2_accT", [128, DM], BF16, st)
        accT = [accT0, accT0]

        t_wgr = kb.dW[0].start([], wgr[:, :].rearrange("p (c f) -> p c f", f=DM),
                               W["wgr"].rearrange("(c p) f -> p c f", p=128))
        t_bg = kb.dW[1].start([], bgr[:, :], W["bg_row"])
        t_be = kb.dW[4].start([], ber[:, :], W["be_flat"])

        pst = kb.scope()
        with pst:
            lg_ps = kb.ps("p2_lg", [128, DM], F32, pst)
            z_ps = [kb.ps(f"p2_z{i}", [128, DM], F32, pst) for i in range(2)]

            ev_tok = None
            wc_tok = None
            for t in range(NTI):
                def f_lg(e, tt=t):
                    last = None
                    for half in range(2):
                        for ch in range(NCH):
                            last = e.matmul(
                                lg_ps[:, half * 512:(half + 1) * 512],
                                A[:, ch * T + tt * 128:ch * T + (tt + 1) * 128],
                                wgr[:, ch * DM + half * 512:ch * DM + (half + 1) * 512],
                                start=(ch == 0), stop=False)
                        last = e.matmul(lg_ps[:, half * 512:(half + 1) * 512],
                                        cons["ones_row"][:, :],
                                        bgr[0:1, half * 512:(half + 1) * 512],
                                        start=False, stop=True)
                    return last

                mm = kb.pe.op([t_wgr, t_bg, ev_tok], f_lg)
                ev_tok = kb.se.op([mm, wc_tok], lambda e: e.activation(
                    lgf[:, :], lg_ps[:, :], AF.Copy))
                nm = kb.ve.op([ev_tok, wc_tok], lambda e: e.reduce_max(
                    pcol[:, :], lgf[:, :], axis=AX, negate=True))
                ez = kb.se.op([nm, ev_tok], lambda e: e.activation(
                    rsc[0][:, :], lgf[:, :], AF.Exp, bias=pcol[:, :],
                    accum_out=zcol[:, :]))
                pz = kb.ve.op([ez], lambda e: e.reciprocal(pcol[:, :], zcol[:, :]))
                m8 = kb.ve.op([pz], lambda e: e.max(f8[:, :], lgf[:, :]))
                ix = kb.ve.op([m8], lambda e: e.max_index(u8[:, :], f8[:, :], lgf[:, :]))
                es = kb.ve.op([ix], lambda e: e.tensor_single_scalar(
                    esel[:, :], u8[:, 0:1], 7, ALU.bitwise_and))
                wc_tok = es
                for ee in range(8):
                    wc_tok = kb.ve.op([wc_tok], lambda e, tt=t, x=ee: e.scalar_tensor_tensor(
                        wcols[:, tt * 8 + x:tt * 8 + x + 1], esel[:, :], float(x),
                        pcol[:, :], ALU.is_equal, ALU.mult))

            we_tok = [None, None]
            mm_last = [None, None]
            add_tok = None
            add_final = {}
            se_tok = [None, None]
            for e_i in range(8):
                par = e_i % 2
                we_tok[par] = kb.dW[2 + par].start(
                    [mm_last[par]],
                    web[par][:, :].rearrange("p (c f) -> p c f", f=DM),
                    W["we"][e_i].rearrange("(c p) f -> p c f", p=128))
                for t in range(NTI):
                    zpar = t % 2

                    def f_z(e, tt=t, pp=par, zz=zpar, ei=e_i):
                        last = None
                        for half in range(2):
                            for ch in range(NCH):
                                last = e.matmul(
                                    z_ps[zz][:, half * 512:(half + 1) * 512],
                                    A[:, ch * T + tt * 128:ch * T + (tt + 1) * 128],
                                    web[pp][:, ch * DM + half * 512:ch * DM + (half + 1) * 512],
                                    start=(ch == 0), stop=False)
                            last = e.matmul(z_ps[zz][:, half * 512:(half + 1) * 512],
                                            cons["ones_row"][:, :],
                                            ber[0:1, ei * DM + half * 512:ei * DM + (half + 1) * 512],
                                            start=False, stop=True)
                        return last

                    mm = kb.pe.op([we_tok[par], t_be, se_tok[zpar], wc_tok], f_z)
                    mm_last[par] = mm
                    if e_i == 0:
                        se_tok[zpar] = kb.se.op([mm], lambda e, tt=t, zz=zpar: e.activation(
                            B[:, tt * DM:(tt + 1) * DM], z_ps[zz][:, :], AF.Relu,
                            scale=wcols[:, tt * 8:tt * 8 + 1]))
                    else:
                        rs = kb.se.op([mm, add_tok], lambda e, tt=t, zz=zpar, ei=e_i, pp=par:
                                      e.activation(rsc[pp][:, :], z_ps[zz][:, :], AF.Relu,
                                                   scale=wcols[:, tt * 8 + ei:tt * 8 + ei + 1]))
                        se_tok[zpar] = rs
                        add_tok = kb.ve.op([rs], lambda e, tt=t, pp=par: e.tensor_tensor(
                            B[:, tt * DM:(tt + 1) * DM], B[:, tt * DM:(tt + 1) * DM],
                            rsc[pp][:, :], ALU.add))
                        if e_i == 7:
                            add_final[t] = add_tok

            res_toks = [None, None]
            for t in range(NTI):
                par = t % 2
                tp_tok = kb.dT[par].start_t(
                    [add_final[t], res_toks[0], res_toks[1]],
                    accT[par][:, :].rearrange("k (c q) -> k c q", q=128),
                    B[:, t * DM:(t + 1) * DM].rearrange("q (c k) -> q c k", k=128))

                def f_res(e, tt=t, pp=par):
                    outap = A.rearrange("p (c t) -> p c t", c=NCH)[:, :, tt * 128:(tt + 1) * 128]
                    return e.scalar_tensor_tensor(
                        outap, accT[pp][:, :].rearrange("p (c t) -> p c t", c=NCH), 1.0,
                        outap, ALU.mult, ALU.add)

                res_toks[par] = kb.ve.op([tp_tok, res_toks[1 - par]], f_res)
    kb.barrier()
    emit_fm_ln(kb, A, tag="sln2")


def p2_weight_drams(nc):
    W = {}
    W["wgr"] = nc.dram_tensor("wgr", [DM, DM], F32, kind="ExternalInput").ap()
    W["bg_row"] = nc.dram_tensor("bg_row", [1, DM], F32, kind="ExternalInput").ap()
    W["we"] = nc.dram_tensor("we", [8, DM, DM], F32, kind="ExternalInput").ap()
    W["be_flat"] = nc.dram_tensor("be_flat", [1, 8 * DM], F32, kind="ExternalInput").ap()
    return W


def p2_weight_maps(inputs):
    return {
        "wgr": np.asarray(inputs["L0_ffn_Wg"], np.float32),
        "bg_row": np.asarray(inputs["L0_ffn_bg"], np.float32).reshape(1, DM),
        "we": np.asarray(inputs["L0_ffn_We"], np.float32),
        "be_flat": np.ascontiguousarray(
            np.asarray(inputs["L0_ffn_be"], np.float32).reshape(1, 8 * DM)),
    }


def build_phase2(dbg=None):
    nc = bass.Bass("TRN2", target_bir_lowering=False)
    cd = declare_const_drams(nc, p1=False)
    s_in = nc.dram_tensor("s_in", [128, NCH * T], BF16, kind="ExternalInput").ap()
    W = p2_weight_drams(nc)
    s_out = nc.dram_tensor("s_out", [128, NCH * T], BF16, kind="ExternalOutput").ap()
    with ExitStack() as ctx:
        kb = KB(nc, ctx)
        A = kb.sb("bufA", [128, NCH * T], BF16)
        B = kb.sb("bufB", [128, NCH * T], BF16)
        load_consts(kb, cd, p1=False)
        kb.dX[1].start([], A[:, :], s_in)
        kb.barrier()
        emit_phase2(kb, A, B, W)
        kb.barrier()
        kb.dS.start([], s_out, A[:, :])
        finish(kb)
    return nc


def emit_phase3(kb, A, W, x2i, x2g):
    cons = kb.consts
    st = kb.scope()
    with st:
        d0 = kb.dX[0].start([], x2i.ap()[:, :], A[:, :])
        cct = kb.collective([d0], "AllGather", ALU.bypass,
                            [x2i.ap().opt()], [x2g.ap().opt()])

        rt = kb.sb("p3_rt", [128, 24 * 128], BF16, st)
        acol = kb.sb("p3_ac", [128, 1], F32, st)
        bcol = kb.sb("p3_bc", [128, 1], F32, st)

        pst0 = kb.scope()
        with pst0:
            rr = kb.sb("p3_rr", [1, DM], F32, pst0)
            n2 = kb.sb("p3_n2", [1, 512], F32, pst0)
            tq = kb.sb("p3_tq", [1, 512], F32, pst0)
            cfull = kb.sb("p3_cf", [1, DM], F32, pst0)
            smfull = kb.sb("p3_sf", [1, DM], F32, pst0)
            ccol = kb.sb("p3_cc", [128, NCH], F32, pst0)
            scol = kb.sb("p3_sc", [128, NCH], F32, pst0)
            sc11 = kb.sb("p3_s11", [1, 2], F32, pst0)
            tcol_ps = kb.ps("p3_tcol", [128, 16], F32, pst0)
            prev = None
            for r in range(3):
                ld = kb.dX[1].start([prev], rr[:, :], W["rotw"][r:r + 1, :])
                t1 = kb.ve.op([ld], lambda e: e.tensor_tensor(
                    n2[:, :], rr[0:1, 0:DM:2], rr[0:1, 0:DM:2], ALU.mult))
                t2 = kb.ve.op([t1], lambda e: e.tensor_tensor(
                    tq[:, :], rr[0:1, 1:DM:2], rr[0:1, 1:DM:2], ALU.mult))
                t3 = kb.ve.op([t2], lambda e: e.tensor_tensor(
                    n2[:, :], n2[:, :], tq[:, :], ALU.add))
                t4 = kb.se.op([t3], lambda e: e.activation(n2[:, :], n2[:, :], AF.Ln))
                t5 = kb.se.op([t4], lambda e: e.activation(n2[:, :], n2[:, :],
                                                           AF.Exp, scale=-0.5))
                c1 = kb.ve.op([t5], lambda e: e.tensor_tensor(
                    tq[:, :], rr[0:1, 0:DM:2], n2[:, :], ALU.mult))
                c2 = kb.ve.op([c1], lambda e: e.tensor_copy(cfull[0:1, 0:DM:2], tq[:, :]))
                c3 = kb.ve.op([c2], lambda e: e.tensor_copy(cfull[0:1, 1:DM:2], tq[:, :]))
                s1 = kb.ve.op([c3], lambda e: e.tensor_tensor(
                    tq[:, :], rr[0:1, 1:DM:2], n2[:, :], ALU.mult))
                s2 = kb.ve.op([s1], lambda e: e.tensor_copy(smfull[0:1, 0:DM:2], tq[:, :]))
                s3 = kb.ve.op([s2], lambda e: e.tensor_scalar_mul(
                    smfull[0:1, 1:DM:2], tq[:, :], -1.0))
                tp1 = kb.pe.op([s3, prev], lambda e: (
                    [e.transpose(tcol_ps[:, ch:ch + 1], cfull[0:1, ch * 128:(ch + 1) * 128],
                                 cons["ones_col_f"][0:1, 0:1]) for ch in range(NCH)]
                    + [e.transpose(tcol_ps[:, 8 + ch:9 + ch], smfull[0:1, ch * 128:(ch + 1) * 128],
                                   cons["ones_col_f"][0:1, 0:1]) for ch in range(NCH)])[-1])
                cc1 = kb.ve.op([tp1], lambda e: e.tensor_copy(ccol[:, :], tcol_ps[:, 0:8]))
                cc2 = kb.ve.op([tp1], lambda e: e.tensor_copy(scol[:, :], tcol_ps[:, 8:16]))
                for ch in range(NCH):
                    e1 = kb.ve.op([cc1, cc2], lambda e, rr_=r, c_=ch: e.tensor_scalar_mul(
                        rt[:, (rr_ * 8 + c_) * 128:(rr_ * 8 + c_ + 1) * 128],
                        cons["eye"][:, :], ccol[:, c_:c_ + 1]))
                    prev = kb.ve.op([e1], lambda e, rr_=r, c_=ch: e.scalar_tensor_tensor(
                        rt[:, (rr_ * 8 + c_) * 128:(rr_ * 8 + c_ + 1) * 128],
                        cons["swap"][:, :], scol[:, c_:c_ + 1],
                        rt[:, (rr_ * 8 + c_) * 128:(rr_ * 8 + c_ + 1) * 128],
                        ALU.mult, ALU.add))
            lsc = kb.dX[1].start([prev], sc11[0:1, 0:1], W["scale"])
            lbi = kb.dX[1].start([lsc], sc11[0:1, 1:2], W["bias"])
            v1 = kb.ve.op([lsc, lbi], lambda e: e.reciprocal(n2[0:1, 0:1], sc11[0:1, 0:1]))
            v2 = kb.ve.op([v1], lambda e: e.tensor_scalar_mul(n2[0:1, 0:1], n2[0:1, 0:1], 2.0))
            v3 = kb.ve.op([v2], lambda e: e.tensor_tensor(
                n2[0:1, 1:2], n2[0:1, 0:1], sc11[0:1, 1:2], ALU.add))
            ab1 = kb.pe.op([v3, tp1], lambda e: (
                e.matmul(tcol_ps[:, 0:1], cons["ones_row_f"][:, :], n2[0:1, 0:1],
                         start=True, stop=True),
                e.matmul(tcol_ps[:, 1:2], cons["ones_row_f"][:, :], n2[0:1, 1:2],
                         start=True, stop=True))[-1])
            ca = kb.ve.op([ab1], lambda e: e.tensor_copy(acol[:, :], tcol_ps[:, 0:1]))
            cb = kb.ve.op([ab1], lambda e: e.tensor_copy(bcol[:, :], tcol_ps[:, 1:2]))
            kb.ve.op([ca, prev], lambda e: e.tensor_scalar_mul(
                rt[:, 0:NCH * 128], rt[:, 0:NCH * 128], acol[:, :]))
        kb.barrier()

        qg = kb.sb("p3_qg", [128, NCH * GRP], BF16, st)
        Og = kb.sb("p3_og", [128, NGT * DM], BF16, st)
        kf3 = kb.sb("p3_kf", [128, NCH * SBK], BF16, st)
        vt3 = kb.sb("p3_vt", [128, (SBK // 128) * DM], BF16, st)
        xk = [kb.sb(f"p3_xk{i}", [128, NCH * SUB], BF16, st) for i in range(2)]
        ptile = kb.sb("p3_pt", [128, SBK], BF16, st)
        ptT = [kb.sb(f"p3_ptt{i}", [128, SBK], BF16, st) for i in range(2)]
        tsf = kb.sb("p3_tsf", [128, SUB], F32, st)
        junk = kb.sb("p3_junk", [128, DM], BF16, st)
        nmst = kb.sb("p3_nmst", [128, NGT], F32, st)
        lst = kb.sb("p3_lst", [128, NGT], F32, st)
        nmj = kb.sb("p3_nmj", [128, 2], F32, st)
        psj = kb.sb("p3_psj", [128, 2], F32, st)
        fj = kb.sb("p3_fj", [128, 2], F32, st)
        alph = kb.sb("p3_alph", [128, 2], F32, st)
        nmnew = kb.sb("p3_nmn", [128, 1], F32, st)
        wv = kb.sb("p3_wv", [128, 2], F32, st)
        wsum = kb.sb("p3_wsum", [128, 1], F32, st)
        s2c = kb.sb("p3_s2c", [128, 1], F32, st)

        pst = kb.scope()
        with pst:
            bd_ps = [kb.ps(f"p3_bd{i}", [128, 512], F32, pst) for i in range(4)]
            vp_ps = kb.ps("p3_vp", [128, DM], F32, pst)

            NKT = SBK // 128
            for g in range(2):
                kb.barrier()
                ev_q = None
                for blk in range(GRP // 512):
                    for ch in range(NCH):
                        par = (blk * NCH + ch) % 2
                        mmq = kb.pe.op([cct, ev_q], lambda e, b=blk, c=ch, pp=par, gg=g: e.matmul(
                            bd_ps[pp][:, :], rt[:, c * 128:(c + 1) * 128],
                            A[:, c * T + gg * GRP + b * 512:c * T + gg * GRP + (b + 1) * 512],
                            start=True, stop=True))
                        ev_q = kb.se.op([mmq], lambda e, b=blk, c=ch, pp=par: e.activation(
                            qg[:, c * GRP + b * 512:c * GRP + (b + 1) * 512],
                            bd_ps[pp][:, :], AF.Copy))
                z1 = kb.ve.op([], lambda e: e.memset(Og[:, :], 0.0))
                z2 = kb.ve.op([z1], lambda e: e.memset(nmst[:, :], 3e38))
                z3 = kb.ve.op([z2], lambda e: e.memset(lst[:, :], 0.0))
                o_up = z3
                ptT_use = None
                vp_use = [None, None]
                ex_pair = [None, None]
                ex_pair_by = [None, None]
                vch3 = None
                mpv_by = [None, None]
                w3_by = [None, None]
                for sb in range(NSB):
                    kv_toks = []
                    for j in range(SBK // SUB):
                        key0 = sb * SBK + j * SUB
                        rank, tt0 = key0 // T, key0 % T
                        xsrc = x2g.ap()[rank * 128:(rank + 1) * 128, :] \
                            .rearrange("p (c t) -> p c t", t=T)[:, :, tt0:tt0 + SUB]
                        lx = kb.dX[j % 2].start(
                            [o_up], xk[j % 2][:, :].rearrange("p (c t) -> p c t", t=SUB), xsrc)
                        ev_k = None
                        for ch in range(NCH):
                            par = ch % 2
                            mmk = kb.pe.op([lx, ev_k, o_up, ex_pair_by[0], ex_pair_by[1]],
                                           lambda e, c=ch, pp=par, jj=j: e.matmul(
                                bd_ps[pp][:, :], rt[:, (8 + c) * 128:(9 + c) * 128],
                                xk[jj % 2][:, c * SUB:(c + 1) * SUB], start=True, stop=True))
                            ev_k = kb.se.op([mmk], lambda e, c=ch, pp=par, jj=j: e.activation(
                                kf3[:, c * SBK + jj * SUB:c * SBK + jj * SUB + SUB],
                                bd_ps[pp][:, :], AF.Copy))
                        kv_toks.append(ev_k)
                        for ks in range(SUB // 128):
                            def f_v(e, jj=j, kk=ks):
                                last = None
                                for c in range(NCH):
                                    last = e.matmul(
                                        vp_ps[:, c * 128:(c + 1) * 128],
                                        xk[jj % 2][:, c * SUB + kk * 128:c * SUB + (kk + 1) * 128],
                                        rt[:, (16 + c) * 128:(17 + c) * 128],
                                        start=True, stop=True)
                                return last

                            mmv = kb.pe.op([lx, o_up, vp_use[0]], f_v)
                            kt = j * (SUB // 128) + ks
                            ev_v = kb.ve.op([mmv], lambda e, k_=kt: e.tensor_copy(
                                vt3[:, k_ * DM:(k_ + 1) * DM], vp_ps[:, :]))
                            vp_use[0] = ev_v
                            kv_toks.append(ev_v)
                    pt_toks = {}
                    for qt in range(NGT + 1):
                        if qt < NGT:
                            pb = qt % 2

                            def f_s(e, q_=qt, b_=pb):
                                last = None
                                for j in range(2):
                                    for c in range(NCH):
                                        last = e.matmul(
                                            bd_ps[2 * b_ + j][:, :],
                                            qg[:, c * GRP + q_ * 128:c * GRP + (q_ + 1) * 128],
                                            kf3[:, c * SBK + j * SUB:c * SBK + (j + 1) * SUB],
                                            start=(c == 0), stop=(c == NCH - 1))
                                return last

                            mms = kb.pe.op(kv_toks + [ex_pair_by[pb], ev_q], f_s)
                            nm0 = kb.ve.op([mms, vch3, ex_pair_by[0], ex_pair_by[1]],
                                           lambda e, b_=pb: e.reduce_max(
                                nmj[:, 0:1], bd_ps[2 * b_][:, :], axis=AX, negate=True))
                            nm1 = kb.ve.op([nm0], lambda e, b_=pb: e.reduce_max(
                                nmj[:, 1:2], bd_ps[2 * b_ + 1][:, :], axis=AX, negate=True))
                            u1 = kb.ve.op([nm1], lambda e: e.tensor_tensor(
                                nmnew[:, :], nmj[:, 0:1], nmj[:, 1:2], ALU.min))
                            u2 = kb.ve.op([u1], lambda e, q_=qt: e.tensor_tensor(
                                nmnew[:, :], nmnew[:, :], nmst[:, q_:q_ + 1], ALU.min))
                            f2 = kb.se.op([u2, o_up], lambda e, q_=qt, b_=pb: e.activation(
                                alph[:, b_:b_ + 1], nmst[:, q_:q_ + 1], AF.Exp, scale=-1.0,
                                bias=nmnew[:, :]))
                            su = kb.ve.op([u2, f2], lambda e, q_=qt: e.tensor_copy(
                                nmst[:, q_:q_ + 1], nmnew[:, :]))
                            ex0 = kb.se.op([u2, pt_toks.get(qt - 1)], lambda e, b_=pb: e.activation(
                                ptile[:, 0:SUB], bd_ps[2 * b_][:, :], AF.Exp,
                                bias=nmnew[:, :], accum_out=psj[:, 0:1]))
                            ex1 = kb.se.op([ex0], lambda e, b_=pb: e.activation(
                                ptile[:, SUB:SBK], bd_ps[2 * b_ + 1][:, :], AF.Exp,
                                bias=nmnew[:, :], accum_out=psj[:, 1:2]))
                            ex_pair_by[pb] = ex1
                            vch3 = su
                            w2 = kb.ve.op([ex1, su], lambda e: e.reduce_sum(
                                wsum[:, :], psj[:, :], axis=AX))
                            w3 = kb.ve.op([w2, f2], lambda e, q_=qt, b_=pb: e.scalar_tensor_tensor(
                                lst[:, q_:q_ + 1], lst[:, q_:q_ + 1], alph[:, b_:b_ + 1],
                                wsum[:, :], ALU.mult, ALU.add))
                            w3_by[pb] = w3
                            pt_toks[qt] = kb.dT[0].start_t(
                                [ex0, ex1, mpv_by[pb]],
                                ptT[pb][:, :].rearrange("k (c q) -> k c q", q=128),
                                ptile[:, :].rearrange("q (c k) -> q c k", k=128))
                        if qt >= 1:
                            q0 = qt - 1
                            b0 = q0 % 2

                            def f_pv(e, b_=b0):
                                last = None
                                for kt in range(NKT):
                                    for half in range(2):
                                        last = e.matmul(
                                            vp_ps[:, half * 512:(half + 1) * 512],
                                            ptT[b_][:, kt * 128:(kt + 1) * 128],
                                            vt3[:, kt * DM + half * 512:kt * DM + (half + 1) * 512],
                                            start=(kt == 0), stop=(kt == NKT - 1))
                                return last

                            mpv = kb.pe.op([pt_toks[q0], o_up, vp_use[0]], f_pv)
                            mpv_by[b0] = mpv
                            o_up = kb.ve.op([mpv, w3_by[b0], o_up], lambda e, q_=q0, b_=b0:
                                            e.scalar_tensor_tensor(
                                Og[:, q_ * DM:(q_ + 1) * DM], Og[:, q_ * DM:(q_ + 1) * DM],
                                alph[:, b_:b_ + 1], vp_ps[:, :], ALU.mult, ALU.add))
                            vp_use[0] = o_up
                ep = None
                for qt in range(NGT):
                    l1 = kb.ve.op([o_up, ep], lambda e, q_=qt: e.reciprocal(
                        s2c[:, :], lst[:, q_:q_ + 1]))
                    q1 = kb.se.op([l1], lambda e, q_=qt: e.activation(
                        junk[:, :], Og[:, q_ * DM:(q_ + 1) * DM], AF.Square,
                        accum_out=wsum[:, :]))
                    i1 = kb.ve.op([q1], lambda e: e.tensor_tensor(
                        wsum[:, :], wsum[:, :], s2c[:, :], ALU.mult))
                    i2 = kb.ve.op([i1], lambda e: e.tensor_tensor(
                        wsum[:, :], wsum[:, :], s2c[:, :], ALU.mult))
                    i3 = kb.ve.op([i2], lambda e: e.tensor_scalar_max(
                        wsum[:, :], wsum[:, :], 1e-8))
                    i4 = kb.se.op([i3], lambda e: e.activation(wsum[:, :], wsum[:, :], AF.Ln))
                    i5 = kb.se.op([i4], lambda e: e.activation(wsum[:, :], wsum[:, :],
                                                               AF.Exp, scale=-0.5))
                    i6 = kb.ve.op([i5], lambda e: e.tensor_tensor(
                        s2c[:, :], s2c[:, :], wsum[:, :], ALU.mult))
                    a1 = kb.ve.op([i6], lambda e, q_=qt: e.tensor_scalar_mul(
                        Og[:, q_ * DM:(q_ + 1) * DM], Og[:, q_ * DM:(q_ + 1) * DM],
                        s2c[:, :]))

                    tpe = kb.dT[1].start_t(
                        [a1, ep, mpv_by[0], mpv_by[1]],
                        ptT[0][:, 0:DM].rearrange("k (c q) -> k c q", q=128),
                        Og[:, qt * DM:(qt + 1) * DM].rearrange("q (c k) -> q c k", k=128))

                    def f_r(e, q_=qt, gg=g):
                        outap = A.rearrange("p (c t) -> p c t", c=NCH)[
                            :, :, gg * GRP + q_ * 128:gg * GRP + (q_ + 1) * 128]
                        return e.scalar_tensor_tensor(
                            outap, ptT[0][:, 0:DM].rearrange("p (c t) -> p c t", c=NCH), 1.0,
                            outap, ALU.mult, ALU.add)

                    ep = kb.ve.op([tpe], f_r)
                ptT_use = ep
    kb.barrier()
    emit_fm_ln(kb, A, tag="sln3")


def p3_weight_drams(nc):
    return {
        "rotw": nc.dram_tensor("rotw", [3, DM], F32, kind="ExternalInput").ap(),
        "scale": nc.dram_tensor("scale_in", [1, 1], F32, kind="ExternalInput").ap(),
        "bias": nc.dram_tensor("bias_in", [1, 1], F32, kind="ExternalInput").ap(),
    }


def p3_weight_maps(inputs):
    return {
        "rotw": np.asarray(inputs["L1_attn_rot_w"], np.float32),
        "scale_in": np.asarray(inputs["L1_attn_scale"], np.float32).reshape(1, 1),
        "bias_in": np.asarray(inputs["L1_attn_bias"], np.float32).reshape(1, 1),
    }


def build_phase3(dbg=None):
    nc = bass.Bass("TRN2", target_bir_lowering=False)
    cd = declare_const_drams(nc, p1=False)
    s_in = nc.dram_tensor("s_in", [128, NCH * T], BF16, kind="ExternalInput").ap()
    W = p3_weight_drams(nc)
    s_out = nc.dram_tensor("s_out", [128, NCH * T], BF16, kind="ExternalOutput").ap()
    x2i = nc.dram_tensor("x2i", [128, NCH * T], BF16)
    x2g = nc.dram_tensor("x2g", [256, NCH * T], BF16)
    with ExitStack() as ctx:
        kb = KB(nc, ctx)
        A = kb.sb("bufA", [128, NCH * T], BF16)
        load_consts(kb, cd, p1=False)
        kb.dX[1].start([], A[:, :], s_in)
        kb.barrier()
        emit_phase3(kb, A, W, x2i, x2g)
        kb.barrier()
        kb.dS.start([], s_out, A[:, :])
        finish(kb)
    return nc


def emit_phase4(kb, A, W, acc_dram, y_out):
    cons = kb.consts
    FH = 2048
    NQ4 = FH // 256
    TH = T // 2
    NTC = TH // 128
    NTB = TH // 512
    st = kb.scope()
    with st:
        wrg = kb.sb("p4_wrg", [128, NCH * 16], BF16, st)
        rl = kb.sb("p4_rl", [128, 16], F32, st)
        nmr = kb.sb("p4_nmr", [128, 1], F32, st)
        zr = kb.sb("p4_zr", [128, 1], F32, st)
        er = kb.sb("p4_er", [128, 16], F32, st)
        cwc = kb.sb("p4_cwc", [128, NTI * 8], F32, st)
        h = kb.sb("p4_h", [128, 16 * TH], BF16, st)
        wd_sb = kb.sb("p4_wd", [128, 16 * DM], BF16, st)
        wgq = [kb.sb(f"p4_wg{i}", [128, NCH * 256], BF16, st) for i in range(2)]
        wuq = [kb.sb(f"p4_wu{i}", [128, NCH * 256], BF16, st) for i in range(2)]
        gsc = [kb.sb(f"p4_gs{i}", [128, 512], BF16, st) for i in range(2)]
        stage = [kb.sb(f"p4_st{i}", [128, DM], F32, st) for i in range(2)]
        acc_sb = kb.sb("p4_acc", [128, DM], F32, st)
        mucol = kb.sb("p4_mu", [128, 1], F32, st)
        sqcol = kb.sb("p4_sq", [128, 1], F32, st)
        vcol = kb.sb("p4_vc", [128, 1], F32, st)

        t_wrg = kb.dW[5].start([], wrg[:, :].rearrange("p (c f) -> p c f", f=16),
                               W["wrg"].rearrange("(c p) f -> p c f", p=128))
        pst = kb.scope()
        with pst:
            r_ps = kb.ps("p4_rps", [128, 16], F32, pst)
            tp_ps = kb.ps("p4_tp", [128, DM], BF16, pst)
            g4 = [kb.ps(f"p4_g{i}", [128, 512], F32, pst) for i in range(2)]
            u4 = [kb.ps(f"p4_u{i}", [128, 512], F32, pst) for i in range(2)]

            vch = None
            mm_prev = None
            for t in range(NTI):
                def f_r(e, tt=t):
                    last = None
                    for ch in range(NCH):
                        last = e.matmul(r_ps[:, :],
                                        A[:, ch * T + tt * 128:ch * T + (tt + 1) * 128],
                                        wrg[:, ch * 16:(ch + 1) * 16],
                                        start=(ch == 0), stop=(ch == NCH - 1))
                    return last

                mm = kb.pe.op([t_wrg, vch, mm_prev], f_r)
                mm_prev = mm
                vch = kb.ve.op([mm, vch], lambda e: e.tensor_copy(rl[:, :], r_ps[:, :]))
                vch = kb.ve.op([vch], lambda e: e.reduce_max(
                    nmr[:, :], rl[:, 0:8], axis=AX, negate=True))
                s1 = kb.se.op([vch], lambda e: e.activation(
                    er[:, 0:8], rl[:, 0:8], AF.Exp, bias=nmr[:, :],
                    accum_out=zr[:, :]))
                vch = kb.ve.op([s1], lambda e: e.reciprocal(nmr[:, :], zr[:, :]))
                vch = kb.ve.op([vch], lambda e: e.tensor_scalar_mul(
                    er[:, 0:8], er[:, 0:8], nmr[:, :]))
                vch = kb.ve.op([vch], lambda e: e.reduce_max(
                    nmr[:, :], rl[:, 8:16], axis=AX, negate=True))
                vch = kb.ve.op([vch], lambda e: e.tensor_scalar_mul(
                    nmr[:, :], nmr[:, :], 10.0))
                s2 = kb.se.op([vch], lambda e: e.activation(
                    er[:, 8:16], rl[:, 8:16], AF.Exp, scale=10.0, bias=nmr[:, :],
                    accum_out=zr[:, :]))
                vch = kb.ve.op([s2], lambda e: e.reciprocal(nmr[:, :], zr[:, :]))
                vch = kb.ve.op([vch], lambda e: e.tensor_scalar_mul(
                    er[:, 8:16], er[:, 8:16], nmr[:, :]))
                vch = kb.ve.op([vch], lambda e: e.tensor_tensor(
                    er[:, 0:8], er[:, 0:8], er[:, 8:16], ALU.mult))
                vch = kb.ve.op([vch], lambda e: e.reduce_sum(zr[:, :], er[:, 0:8], axis=AX))
                vch = kb.ve.op([vch], lambda e: e.tensor_single_scalar(
                    zr[:, :], zr[:, :], 1e-9, ALU.add))
                vch = kb.ve.op([vch], lambda e: e.reciprocal(zr[:, :], zr[:, :]))
                vch = kb.ve.op([vch], lambda e, tt=t: e.tensor_scalar_mul(
                    cwc[:, tt * 8:(tt + 1) * 8], er[:, 0:8], zr[:, :]))
            cw_tok = vch
            kb.barrier()

            acc_tok = {}
            prev_store = None
            st_use = [None, None]
            wg_tok = [None, None]
            wu_tok = [None, None]
            wg_used = [None, None]
            up_done = [None, None]
            wd_tok = None
            dn_evac = None
            dn_par = [None, None]
            h_used = None
            pre = {}
            slabs = [(hh, ee, ff) for hh in range(2) for ee in range(8)
                     for ff in range(2)]

            def emit_up_load(ei_, fh_, q4_):
                par_ = q4_ % 2
                qg_ = fh_ * NQ4 + q4_
                g_t = []
                u_t = []
                for hv in range(2):
                    sl = slice(hv * 1024, (hv + 1) * 1024)
                    g_t.append(kb.dW[hv].start(
                        [wg_used[par_]],
                        wgq[par_][:, sl].rearrange("p (c f) -> p c f", f=256),
                        W["wg"][ei_][qg_][:, sl].rearrange("p (c f) -> p c f", f=256)))
                    u_t.append(kb.dW[2 + hv].start(
                        [wg_used[par_]],
                        wuq[par_][:, sl].rearrange("p (c f) -> p c f", f=256),
                        W["wu"][ei_][qg_][:, sl].rearrange("p (c f) -> p c f", f=256)))
                return g_t, u_t

            for half in range(2):
                for e_i in range(8):
                    for fh in range(2):
                        hu = h_used
                        wd_tok = kb.dW[4].start(
                            [dn_evac],
                            wd_sb[:, :].rearrange("p (t c) -> p t c", c=DM),
                            W["wd"][e_i][fh].rearrange("p (t c) -> p t c", c=DM))
                        for q4 in range(NQ4):
                            par = q4 % 2
                            key = (half, e_i, fh, q4)
                            if key in pre:
                                wg_tok[par], wu_tok[par] = pre.pop(key)
                            else:
                                wg_tok[par], wu_tok[par] = emit_up_load(e_i, fh, q4)
                            for ftt in range(2):
                                for tb in range(NTB):
                                    gpar = tb % 2

                                    def f_g(e, f_=ftt, b=tb, pp=par, gp=gpar, hh=half):
                                        last = None
                                        for ch in range(NCH):
                                            last = e.matmul(
                                                g4[gp][:, :],
                                                wgq[pp][:, ch * 256 + f_ * 128:ch * 256 + (f_ + 1) * 128],
                                                A[:, ch * T + hh * TH + b * 512:ch * T + hh * TH + (b + 1) * 512],
                                                start=(ch == 0), stop=(ch == NCH - 1))
                                        return last

                                    def f_u(e, f_=ftt, b=tb, pp=par, gp=gpar, hh=half):
                                        last = None
                                        for ch in range(NCH):
                                            last = e.matmul(
                                                u4[gp][:, :],
                                                wuq[pp][:, ch * 256 + f_ * 128:ch * 256 + (f_ + 1) * 128],
                                                A[:, ch * T + hh * TH + b * 512:ch * T + hh * TH + (b + 1) * 512],
                                                start=(ch == 0), stop=(ch == NCH - 1))
                                        return last

                                    mg = kb.pe.op(list(wg_tok[par]) + [cw_tok, up_done[gpar],
                                                   dn_par[0]], f_g)
                                    mu_ = kb.pe.op(list(wu_tok[par]) + [up_done[gpar],
                                                    dn_par[1]], f_u)
                                    wg_used[par] = mu_
                                    sg = kb.se.op([mg, up_done[gpar]],
                                                  lambda e, gp=gpar: e.activation(
                                        gsc[gp][:, :], g4[gp][:, :], AF.Silu))
                                    hd = kb.ve.op([sg, mu_, hu], lambda e, q_=q4, f_=ftt,
                                                  b_=tb, gp=gpar: e.tensor_tensor(
                                        h[:, (q_ * 2 + f_) * TH + b_ * 512:(q_ * 2 + f_) * TH + (b_ + 1) * 512],
                                        gsc[gp][:, :], u4[gp][:, :], ALU.mult))
                                    up_done[gpar] = hd
                        si = slabs.index((half, e_i, fh))
                        if si + 1 < len(slabs):
                            nh, ne, nf = slabs[si + 1]
                            for q4n in range(2):
                                pre[(nh, ne, nf, q4n)] = emit_up_load(ne, nf, q4n)
                        h_full = [up_done[0], up_done[1]]
                        for tc in range(NTC):
                            spar = tc % 2
                            pp4 = g4 if spar == 0 else u4

                            def f_d(e, t_=tc, hh=half, px=pp4):
                                last = None
                                for hf in range(2):
                                    for ft in range(16):
                                        last = e.matmul(
                                            px[hf][:, :],
                                            h[:, ft * TH + t_ * 128:ft * TH + (t_ + 1) * 128],
                                            wd_sb[:, ft * DM + hf * 512:ft * DM + (hf + 1) * 512],
                                            start=(ft == 0), stop=(ft == 15))
                                return last

                            md = kb.pe.op(h_full + [wd_tok, dn_par[spar]], f_d)
                            h_used = md
                            gt = (half, tc)
                            for hf in range(2):
                                dn_par[spar] = kb.ve.op([md, st_use[spar]], lambda e, t_=tc,
                                                        hh=half, sp=spar, ei=e_i, f_=hf,
                                                        px=pp4: e.tensor_scalar_mul(
                                    stage[sp][:, f_ * 512:(f_ + 1) * 512], px[f_][:, :],
                                    cwc[:, (hh * NTC + t_) * 8 + ei:(hh * NTC + t_) * 8 + ei + 1]))
                            dn_evac = dn_par[spar]
                            first = (e_i == 0 and fh == 0)
                            st_use[spar] = kb.dA[spar].start(
                                [dn_evac, acc_tok.get(gt)],
                                acc_dram.ap()[half * TH + tc * 128:half * TH + (tc + 1) * 128, :],
                                stage[spar][:, :],
                                **({} if first else {"accum_op": ALU.add}))
                            acc_tok[gt] = st_use[spar]
                tl = prev_store
                s1_prev = None
                for tc in range(NTC):
                    ld = kb.dX[tc % 2].start([acc_tok[(half, tc)], tl],
                                             acc_sb[:, :],
                                             acc_dram.ap()[half * TH + tc * 128:half * TH + (tc + 1) * 128, :])

                    def f_t(e, t_=tc, hh=half):
                        last = None
                        for c in range(NCH):
                            last = e.transpose(
                                tp_ps[:, c * 128:(c + 1) * 128],
                                A[:, c * T + hh * TH + t_ * 128:c * T + hh * TH + (t_ + 1) * 128],
                                cons["eye"][:, :])
                        return last

                    tpe = kb.pe.op([tl, dn_evac, s1_prev], f_t)
                    s1 = kb.ve.op([ld, tpe, tl], lambda e: e.tensor_tensor(
                        acc_sb[:, :], acc_sb[:, :], tp_ps[:, :], ALU.add))
                    s1_prev = s1
                    s2 = kb.ve.op([s1], lambda e: e.reduce_sum(mucol[:, :], acc_sb[:, :], axis=AX))
                    s3 = kb.se.op([s1, h_used], lambda e: e.activation(
                        h[:, 0:DM], acc_sb[:, :], AF.Square, accum_out=sqcol[:, :]))
                    h_used = s3
                    s4 = kb.ve.op([s2], lambda e: e.tensor_scalar_mul(
                        mucol[:, :], mucol[:, :], 1.0 / DM))
                    s5 = kb.ve.op([s4, s3], lambda e: e.tensor_scalar(
                        sqcol[:, :], sqcol[:, :], 1.0 / DM, EPS, ALU.mult, ALU.add))
                    s6 = kb.ve.op([s5], lambda e: e.tensor_tensor(
                        vcol[:, :], mucol[:, :], mucol[:, :], ALU.mult))
                    s7 = kb.ve.op([s6], lambda e: e.tensor_tensor(
                        vcol[:, :], sqcol[:, :], vcol[:, :], ALU.subtract))
                    s8 = kb.se.op([s7], lambda e: e.activation(vcol[:, :], vcol[:, :], AF.Ln))
                    s9 = kb.se.op([s8], lambda e: e.activation(vcol[:, :], vcol[:, :],
                                                               AF.Exp, scale=-0.5))
                    sa = kb.ve.op([s9, s3], lambda e: e.tensor_scalar(
                        acc_sb[:, :], acc_sb[:, :], mucol[:, :], vcol[:, :],
                        ALU.subtract, ALU.mult))
                    tl = kb.dS.start([sa], y_out[half * TH + tc * 128:half * TH + (tc + 1) * 128, :],
                                     acc_sb[:, :])
                    prev_store = tl
    kb.barrier()


def p4_weight_drams(nc):
    return {
        "wrg": nc.dram_tensor("wrg", [DM, 16], F32, kind="ExternalInput").ap(),
        "wg": nc.dram_tensor("wg4", [8, 16, 128, 2048], F32, kind="ExternalInput").ap(),
        "wu": nc.dram_tensor("wu4", [8, 16, 128, 2048], F32, kind="ExternalInput").ap(),
        "wd": nc.dram_tensor("wd4", [8, 2, 128, 16 * DM], F32, kind="ExternalInput").ap(),
    }


def _p4_reorder_up(w):
    a = np.asarray(w, np.float32).reshape(8, 8, 128, 16, 256)
    return np.ascontiguousarray(a.transpose(0, 3, 2, 1, 4).reshape(8, 16, 128, 2048))


def _p4_reorder_dn(w):
    a = np.asarray(w, np.float32).reshape(8, 2, 16, 128, DM)
    return np.ascontiguousarray(a.transpose(0, 1, 3, 2, 4).reshape(8, 2, 128, 16 * DM))


def p4_weight_maps(inputs):
    return {
        "wrg": np.ascontiguousarray(np.concatenate(
            [np.asarray(inputs["L1_ffn_w_router"], np.float32),
             np.asarray(inputs["L1_ffn_w_gating"], np.float32)], axis=1)),
        "wg4": _p4_reorder_up(inputs["L1_ffn_w_gate"]),
        "wu4": _p4_reorder_up(inputs["L1_ffn_w_up"]),
        "wd4": _p4_reorder_dn(inputs["L1_ffn_w_down"]),
    }


def build_phase4(dbg=None):
    nc = bass.Bass("TRN2", target_bir_lowering=False)
    cd = declare_const_drams(nc, p1=False)
    s_in = nc.dram_tensor("s_in", [128, NCH * T], BF16, kind="ExternalInput").ap()
    W = p4_weight_drams(nc)
    y_out = nc.dram_tensor("y_out", [T, DM], F32, kind="ExternalOutput").ap()
    acc_dram = nc.dram_tensor("acc4", [T, DM], F32)
    with ExitStack() as ctx:
        kb = KB(nc, ctx)
        A = kb.sb("bufA", [128, NCH * T], BF16)
        load_consts(kb, cd, p1=False)
        kb.dX[1].start([], A[:, :], s_in)
        kb.barrier()
        emit_phase4(kb, A, W, acc_dram, y_out)
        finish(kb)
    return nc


def p1_weight_drams(nc):
    W = {}
    for name, shape in [("wqkv", [PAIRS, 128, 3 * DM]),
                        ("wo", [DM, DM]), ("wg1", [DM, 64]), ("wg2", [64, DM]),
                        ("gammabeta", [128, 16])]:
        W[name] = nc.dram_tensor(name, shape, F32, kind="ExternalInput").ap()
    return W


def p1_weight_maps(inputs):
    g = np.asarray(inputs["L0_attn_gamma"], np.float32).reshape(NCH, 128).T
    b = np.asarray(inputs["L0_attn_beta"], np.float32).reshape(NCH, 128).T

    def slab(key):
        w = np.asarray(inputs[key], np.float32).reshape(8, 128, 8, 128)
        return w.transpose(2, 1, 0, 3)

    wqkv = np.stack([slab("L0_attn_Wq"), slab("L0_attn_Wk"), slab("L0_attn_Wv")],
                    axis=2)
    return {
        "wqkv": np.ascontiguousarray(wqkv.reshape(PAIRS, 128, 3 * DM)),
        "wo": np.asarray(inputs["L0_attn_Wo"], np.float32),
        "wg1": np.asarray(inputs["L0_attn_Wg1"], np.float32),
        "wg2": np.asarray(inputs["L0_attn_Wg2"], np.float32),
        "gammabeta": np.ascontiguousarray(np.concatenate([g, b], axis=1)),
    }


def finish(kb, out_tok=None):
    toks = kb.barrier()

    def body(e):
        kb.sy._waits(e, toks)
        e.wait_ge(kb.dS.ctr.h, kb.dS.ctr.n)

    kb.sy._body(body)


def build_phase1(dbg=None):
    nc = bass.Bass("TRN2", target_bir_lowering=False)
    cd = declare_const_drams(nc)
    x0 = nc.dram_tensor("x0", [T, DM], F32, kind="ExternalInput").ap()
    W = p1_weight_drams(nc)
    s_out = nc.dram_tensor("s_out", [128, NCH * T], BF16, kind="ExternalOutput").ap()
    dbg_ap = None
    if dbg:
        dbg_ap = nc.dram_tensor("dbg_o", [128, NCH * T], BF16, kind="ExternalOutput").ap()
    kv_in = [nc.dram_tensor(f"kvi{p}", [64, 128], F32) for p in range(PAIRS)]
    kv_out = [nc.dram_tensor(f"kvo{p}", [128, 128], F32) for p in range(PAIRS)]
    with ExitStack() as ctx:
        kb = KB(nc, ctx)
        A = kb.sb("bufA", [128, NCH * T], BF16)
        B = kb.sb("bufB", [128, NCH * T], BF16)
        load_consts(kb, cd)
        kb.barrier()
        emit_load_x_fm(kb, A, x0)
        emit_phase1(kb, A, B, W, kv_in, kv_out)
        if dbg:
            kb.dS.start([], dbg_ap, B[:, :])
        emit_phase1_tail(kb, A, B, W)
        kb.barrier()
        kb.dS.start([], s_out, A[:, :])
        finish(kb)
    return nc


def fm_to_tm(arr):
    a = np.asarray(arr, BF).astype(np.float32).reshape(128, NCH, T)
    return a.transpose(2, 1, 0).reshape(T, NCH * 128)


def tm_to_fm_f32(x):
    return np.ascontiguousarray(
        x.reshape(T, NCH, 128).transpose(2, 1, 0).reshape(128, NCH * T))


def build_full():
    nc = bass.Bass("TRN2", target_bir_lowering=False)
    cd = declare_const_drams(nc, p1=True)
    x0 = nc.dram_tensor("x0", [T, DM], F32, kind="ExternalInput").ap()
    W1 = p1_weight_drams(nc)
    W2 = p2_weight_drams(nc)
    W3 = p3_weight_drams(nc)
    W4 = p4_weight_drams(nc)
    y_out = nc.dram_tensor("y_out", [T, DM], F32, kind="ExternalOutput").ap()
    kv_in = [nc.dram_tensor(f"kvi{p}", [64, 128], F32) for p in range(PAIRS)]
    kv_out = [nc.dram_tensor(f"kvo{p}", [128, 128], F32) for p in range(PAIRS)]
    x2i = nc.dram_tensor("x2i", [128, NCH * T], BF16)
    x2g = nc.dram_tensor("x2g", [256, NCH * T], BF16)
    acc_dram = nc.dram_tensor("acc4", [T, DM], F32)
    with ExitStack() as ctx:
        kb = KB(nc, ctx)
        A = kb.sb("bufA", [128, NCH * T], BF16)
        load_consts(kb, cd, p1=False)
        kb.barrier()
        emit_load_x_fm(kb, A, x0)
        stB = kb.scope()
        with stB:
            B = kb.sb("bufB", [128, NCH * T], BF16, stB)
            st1c = kb.scope()
            with st1c:
                load_p1_consts(kb, cd, st1c)
                kb.barrier()
                emit_phase1(kb, A, B, W1, kv_in, kv_out)
                emit_phase1_tail(kb, A, B, W1)
            kb.barrier()
            emit_phase2(kb, A, B, W2)
        kb.barrier()
        emit_phase3(kb, A, W3, x2i, x2g)
        emit_phase4(kb, A, W4, acc_dram, y_out)
        finish(kb)
    return nc


def kernel(**inputs):
    x0_full = np.ascontiguousarray(
        np.asarray(inputs["L0_attn_x"], np.float32).reshape(8 * T, DM))
    nc = build_full()
    w = {}
    w.update(p1_weight_maps(inputs))
    w.update(p2_weight_maps(inputs))
    w.update(p3_weight_maps(inputs))
    w.update(p4_weight_maps(inputs))
    maps = []
    for c in range(8):
        m = dict(const_in_maps(c, p1=True))
        m.update(w)
        m["x0"] = np.ascontiguousarray(x0_full[c * T:(c + 1) * T])
        maps.append(m)
    res = run_bass_kernel_spmd(nc, maps, core_ids=list(range(8)))
    y = np.concatenate([np.asarray(res.results[c]["y_out"]) for c in range(8)], axis=0)
    return y.reshape(4, 8192, 1024)


def kernel_unfused(**inputs):
    x0_full = np.ascontiguousarray(
        np.asarray(inputs["L0_attn_x"], np.float32).reshape(8 * T, DM))

    nc1 = build_phase1()
    w1 = p1_weight_maps(inputs)
    maps = []
    for c in range(8):
        m = dict(const_in_maps(c))
        m.update(w1)
        m["x0"] = np.ascontiguousarray(x0_full[c * T:(c + 1) * T])
        maps.append(m)
    r1 = run_bass_kernel_spmd(nc1, maps, core_ids=list(range(8)))
    streams = [np.asarray(r1.results[c]["s_out"]) for c in range(8)]

    nc2 = build_phase2()
    w2 = p2_weight_maps(inputs)
    maps = []
    for c in range(8):
        m = dict(const_in_maps(c, p1=False))
        m.update(w2)
        m["s_in"] = streams[c]
        maps.append(m)
    r2 = run_bass_kernel_spmd(nc2, maps, core_ids=list(range(8)))
    streams = [np.asarray(r2.results[c]["s_out"]) for c in range(8)]

    nc3 = build_phase3()
    w3 = p3_weight_maps(inputs)
    maps = []
    for c in range(8):
        m = dict(const_in_maps(c, p1=False))
        m.update(w3)
        m["s_in"] = streams[c]
        maps.append(m)
    r3 = run_bass_kernel_spmd(nc3, maps, core_ids=list(range(8)))
    streams = [np.asarray(r3.results[c]["s_out"]) for c in range(8)]

    nc4 = build_phase4()
    w4 = p4_weight_maps(inputs)
    maps = []
    for c in range(8):
        m = dict(const_in_maps(c, p1=False))
        m.update(w4)
        m["s_in"] = streams[c]
        maps.append(m)
    r4 = run_bass_kernel_spmd(nc4, maps, core_ids=list(range(8)))
    y = np.concatenate([np.asarray(r4.results[c]["y_out"]) for c in range(8)], axis=0)
    return y.reshape(4, 8192, 1024)
